# version 10
# baseline (speedup 1.0000x reference)
import numpy as np
import concourse.bass as bass
import concourse.mybir as mybir
from concourse.bass_utils import run_bass_kernel_spmd

F32 = mybir.dt.float32
BF16 = mybir.dt.bfloat16
AF = mybir.ActivationFunctionType
ALU = mybir.AluOpType
AX = mybir.AxisListType

D = 1024
KC = 8
HEADS = 16
HD = 64
NG = 4
E = 8
DFF = 4096
NDFF = 32
LAYERS = 2
NEGBIG = -30000.0
SB_CAP = 212000
N_CORES = 8
T_FULL = 2048

TAP_SPEC = []
STOP_AFTER = None


class _StopBuild(Exception):
    pass


class View:
    __slots__ = ("ap", "space", "p0", "p1", "b0", "b1")

    def __init__(self, ap, space, p0, p1, b0, b1):
        self.ap, self.space = ap, space
        self.p0, self.p1, self.b0, self.b1 = p0, p1, b0, b1


class Tile:
    def __init__(self, handle, rows, cols, dtype, offset, space):
        self.h, self.rows, self.cols = handle, rows, cols
        self.dtype = dtype
        self.esz = 2 if dtype == BF16 else 4
        self.off, self.space = offset, space

    def __getitem__(self, key):
        ps, fs = key
        if isinstance(ps, slice):
            p0 = ps.start or 0
            p1 = ps.stop if ps.stop is not None else self.rows
        else:
            p0, p1 = ps, ps + 1
        if isinstance(fs, slice):
            f0 = fs.start or 0
            f1 = fs.stop if fs.stop is not None else self.cols
        else:
            f0, f1 = fs, fs + 1
        return View(self.h[p0:p1, f0:f1], self.space, p0, p1,
                    self.off + f0 * self.esz, self.off + f1 * self.esz)

    @property
    def v(self):
        return self[:, :]


class Op:
    __slots__ = ("idx", "eng", "fn", "reads", "writes", "is_dma", "family",
                 "inc_needed", "edges")

    def __init__(self, idx, eng, fn, reads, writes, is_dma, family):
        self.idx, self.eng, self.fn = idx, eng, fn
        self.reads, self.writes = reads, writes
        self.is_dma, self.family = is_dma, family
        self.inc_needed = False
        self.edges = []


class Prog:
    ENGINES = ("sync", "tensor", "vector", "scalar", "gpsimd")

    def __init__(self, nc):
        self.nc = nc
        self.ops = []
        self.mem = {}
        self.last_fam = {}

    def psum(self, name):
        h = self.nc.alloc_psum_tensor(name, [128, 512], F32)
        return Tile(h, 128, 512, F32, 0, "PS_" + name)

    @staticmethod
    def _ovl(a, b):
        return not (a.p1 <= b.p0 or b.p1 <= a.p0 or a.b1 <= b.b0 or b.b1 <= a.b0)

    def _record(self, op):
        deps = {}
        for r in op.reads:
            for rect, p, is_w in self.mem.get(r.space, ()):
                if is_w and self._ovl(r, rect):
                    deps[p.idx] = p
        for w in op.writes:
            for rect, p, is_w in self.mem.get(w.space, ()):
                if self._ovl(w, rect):
                    if is_w or p.eng != op.eng or p.is_dma or op.is_dma:
                        deps[p.idx] = p
        for idx in sorted(deps):
            p = deps[idx]
            if (p.eng == "tensor" and op.eng == "tensor"
                    and not p.is_dma and not op.is_dma):
                continue
            op.edges.append(p)
        for w in op.writes:
            lst = self.mem.setdefault(w.space, [])
            lst[:] = [rec for rec in lst
                      if not (w.p0 <= rec[0].p0 and rec[0].p1 <= w.p1
                              and w.b0 <= rec[0].b0 and rec[0].b1 <= w.b1)]
            lst.append((w, op, True))
        for r in op.reads:
            self.mem.setdefault(r.space, []).append((r, op, False))

    def op(self, eng, fn, reads=(), writes=()):
        o = Op(len(self.ops), eng, fn, list(reads), list(writes), False, None)
        self.ops.append(o)
        self._record(o)
        return o

    def dma(self, eng, out, in_, family, **kw):
        out_ap = out.ap if isinstance(out, View) else out
        in_ap = in_.ap if isinstance(in_, View) else in_
        reads = [in_] if isinstance(in_, View) else []
        writes = [out] if isinstance(out, View) else []
        fn = (lambda e, o=out_ap, i=in_ap, k=kw: e.dma_start(out=o, in_=i, **k))
        o = Op(len(self.ops), eng, fn, reads, writes, True, family)
        self.ops.append(o)
        self._record(o)
        prev = self.last_fam.get(family)
        if prev is not None:
            o.edges.append(prev)
        self.last_fam[family] = o
        return o

    def dram_touch(self, op_obj, space, write):
        rect = View(None, space, 0, 1 << 20, 0, 1 << 40)
        for r0, p, is_w in list(self.mem.get(space, ())):
            if (is_w or write) and p is not op_obj:
                op_obj.edges.append(p)
        self.mem.setdefault(space, []).append((rect, op_obj, write))

    def build(self, final_wait_families):
        nc = self.nc
        for o in self.ops:
            for p in o.edges:
                p.inc_needed = True
        eng_counts = {e: 0 for e in self.ENGINES}
        fam_counts = {}
        opval = {}
        for o in self.ops:
            if o.is_dma:
                fam_counts[o.family] = fam_counts.get(o.family, 0) + 1
                opval[o.idx] = ("fam", o.family, fam_counts[o.family] * 16)
            elif o.inc_needed:
                eng_counts[o.eng] += 1
                opval[o.idx] = ("eng", o.eng, eng_counts[o.eng])
        eng_sems = {e: nc.alloc_semaphore(name=f"ch_{e}")
                    for e in self.ENGINES if eng_counts[e]}
        fam_sems = {f: nc.alloc_semaphore(name=f"fm_{f}") for f in fam_counts}
        streams = {e: [] for e in self.ENGINES}
        for o in self.ops:
            streams[o.eng].append(o)

        def emitter(engname):
            def run(eng):
                wm = {}
                for o in streams[engname]:
                    best = {}
                    for p in o.edges:
                        if p.idx not in opval:
                            continue
                        kind, key, val = opval[p.idx]
                        if best.get((kind, key), -1) < val:
                            best[(kind, key)] = val
                    for (kind, key), val in sorted(best.items(),
                                                   key=lambda kv: str(kv[0])):
                        if wm.get((kind, key), -1) >= val:
                            continue
                        sem = fam_sems[key] if kind == "fam" else eng_sems[key]
                        eng.wait_ge(sem, val)
                        wm[(kind, key)] = val
                    inst = o.fn(eng)
                    if o.idx in opval:
                        kind, key, _ = opval[o.idx]
                        sem = fam_sems[key] if kind == "fam" else eng_sems[key]
                        inst.then_inc(sem, 16 if kind == "fam" else 1)
                if engname == "gpsimd":
                    for fam in final_wait_families:
                        if fam in fam_sems:
                            eng.wait_ge(fam_sems[fam], fam_counts[fam] * 16)
            return run

        with nc.allow_low_precision(reason="bf16 compute sanctioned by "
                                      "problem tolerance"), nc.Block() as blk:
            blk.sync(emitter("sync"))
            blk.tensor(emitter("tensor"))
            blk.vector(emitter("vector"))
            blk.scalar(emitter("scalar"))
            blk.gpsimd(emitter("gpsimd"))
        return len(self.ops)


def build_program(T):
    NT = T // 128
    NS = T // 512
    TS = min(1024, T)
    NP = T // TS
    NS2 = TS // 512

    nc = bass.Bass("TRN2", target_bir_lowering=False)
    _probe = nc.alloc_sbuf_tensor("_base_probe", [128, 8], F32)
    SB_BASE = nc.lookup_mloc(_probe).addr
    SB_TOP = SB_BASE + nc.sbuf_bytes_remaining + 32
    P = Prog(nc)
    dram = {}

    def din(name, shape, dtype=F32):
        dram[name] = nc.dram_tensor(name, list(shape), dtype,
                                    kind="ExternalInput").ap()
        return dram[name]

    din("x_fm", (D, T))
    out_fm = nc.dram_tensor("out_fm", [D, T], F32, kind="ExternalOutput").ap()
    din("dn_wqkvg_r", (128, 4 * KC * 128))
    din("dn_bqkvg_r", (128, 4))
    din("dn_vab_r", (128, 9 * 130))
    din("dn_wo_r", (128, KC * 128))
    din("dn_bo_fm", (128, KC))
    din("at_wq_r", (LAYERS, KC, 128, KC * 128))
    din("at_wk_r", (LAYERS, KC, 128, KC * 128))
    din("at_wv9_r", (LAYERS, NG, 2, 128, 9 * 130))
    din("at_wo_r", (LAYERS, 128, KC, 1024))
    din("atb_fm", (128, 3 * LAYERS * KC))
    din("lnp_fm", (128, 4 * LAYERS * KC))
    din("gate_w_fm", (128, LAYERS * KC * E))
    din("gate_b_row", (1, LAYERS * E))
    din("e_w1_r", (LAYERS, E, NDFF, 128, KC * 128))
    din("e_b1_fm", (LAYERS, E, 128, NDFF))
    din("e_w2", (LAYERS, E, DFF, D))
    din("e_b2", (LAYERS, E, D))
    din("c_eye", (128, 128))
    din("c_ut_ones", (128, 128))
    din("c_nb_utinc", (128, 128))
    din("c_nb_ltstrict", (128, 128))
    din("c_onesf", (128, 128))
    din("c_onesb", (128, 512), BF16)
    din("c_causw", (128, 896), BF16)
    din("c_csel", (8, 1024), BF16)
    din("c_col", (128, 8))

    xres_d = nc.dram_tensor("xres_d", [D, T], F32, kind="Internal").ap()
    w1bf_d = nc.dram_tensor("w1bf_d", [LAYERS, E, NDFF, 128, 1024], BF16,
                            kind="Internal").ap()
    w2bf_d = nc.dram_tensor("w2bf_d", [LAYERS, E, NDFF, 128, 1024], BF16,
                            kind="Internal").ap()

    tiles = {}
    auto_ptr = [0]

    def sb(name, rows, cols, dtype, offset=None):
        esz = 2 if dtype == BF16 else 4
        if offset is None:
            offset = (auto_ptr[0] + 31) & ~31
            auto_ptr[0] = offset + cols * esz
        assert offset + cols * esz + SB_BASE <= SB_TOP, (name, offset + cols * esz)
        h = nc.alloc_sbuf_tensor_at(name, [rows, cols], dtype,
                                    offset=offset + SB_BASE)
        t = Tile(h, rows, cols, dtype, offset, "SB")
        tiles[name] = t
        return t

    xa = sb("xa", 128, KC * T, F32, 0)
    BF0 = KC * T * 4
    bfA = sb("bfA", 128, KC * T, BF16, BF0)
    qT = sb("qT", 128, T, F32, BF0)
    gTt = sb("gTt", 128, T, F32, BF0 + T * 4)
    AR = BF0 + KC * T * 2
    ktok = sb("ktok", 128, T, F32, AR)
    uv = sb("uv", 128, T, F32, AR + T * 4)
    wt = sb("wt", 128, T, F32, AR + 2 * T * 4)
    et = sb("et", 128, T, F32, AR + 3 * T * 4)
    hS = sb("hS", 128, T, F32, AR + 4 * T * 4)
    htS = sb("htS", 128, T, F32, AR + 5 * T * 4)
    khat = sb("khat", 128, T, F32, AR + 6 * T * 4)
    vtok = sb("vtok", 128, T, F32, AR + 7 * T * 4)
    qt_g = sb("qt_g", 128, 2 * T, BF16, AR)
    kt_g = sb("kt_g", 128, 2 * T, BF16, AR + 4 * T)
    vt_g = sb("vt_g", 128, NT * 260, BF16, AR + 8 * T)
    og = sb("og", 128, KC * T, BF16, AR + 8 * T + ((NT * 260 * 2 + 31) & ~31))
    h1p = sb("h1p", 128, NDFF * TS, BF16, AR)
    SHR = AR + max(8 * T * 4, NDFF * TS * 2)
    xbf = sb("xbf", 128, 2048, BF16, SHR)
    xsq = sb("xsq", 128, 2048, BF16, SHR + 4096)
    cumr = sb("cumr", 1, 128, F32, SHR)
    Xt2 = [sb(f"Xt2_{i}", 128, 128, F32, SHR + 1024 + i * 512) for i in range(2)]
    Yt2 = [sb(f"Yt2_{i}", 128, 128, F32, SHR + 2048 + i * 512) for i in range(2)]
    Rt2 = [sb(f"Rt2_{i}", 128, 256, F32, SHR + 3072 + i * 1024) for i in range(2)]
    CBt2 = sb("CBt2", 128, 128, F32, SHR + 5120)
    DIF2 = sb("DIF2", 128, 128, F32, SHR + 5632)
    DPS2 = sb("DPS2", 128, 128, F32, SHR + 6144)
    cumr2 = sb("cumr2", 1, 128, F32, SHR + 6656)
    cnb2 = sb("cnb2", 128, 1, F32, SHR + 7168)
    colsc2 = sb("colsc2", 128, 4, F32, SHR + 7232)
    wobf = sb("wobf", 128, 4096, BF16, SHR)
    lgT = sb("lgT", 8, T, F32, SHR)
    pa = [sb(f"pa{i}", 128, 512, BF16, SHR + i * 1024) for i in range(2)]
    pb = [sb(f"pb{i}", 128, 512, BF16, SHR + 2048 + i * 1024) for i in range(2)]
    linv = sb("linv", 1, 512, BF16, SHR + 4096)
    linvb = sb("linvb", 64, 512, F32, SHR + 5152)
    cwT = sb("cwT", 8, T, BF16, SHR)
    cwbT = sb("cwbT", 128, TS, BF16, SHR + 4096)
    gelt = [sb(f"gelt{i}", 128, 512, BF16, SHR + 6144 + i * 1024)
            for i in range(2)]
    ST = SHR + 8192
    w1f = [sb(f"w1f{i}", 128, 512, F32, ST + i * 2048) for i in range(2)]
    w2f = [sb(f"w2f{i}", 128, 512, F32, ST + 4096 + i * 2048) for i in range(2)]
    cvb = [sb(f"cvb{i}", 128, 1024, BF16, ST + 4096 + i * 2048) for i in range(2)]
    w1b = [sb(f"w1b{i}", 128, 1024, BF16, ST + 8192 + i * 2048) for i in range(2)]
    w2b = [sb(f"w2b{i}", 128, 1024, BF16, ST + 12288 + i * 2048) for i in range(2)] \
        + [sb(f"w2b{j+2}", 128, 1024, BF16, ST + 4096 + j * 2048) for j in range(2)]
    awf = [sb(f"awf{i}", 128, 1280, F32, ST + i * 5120) for i in range(2)]
    awb = [sb(f"awb{i}", 128, 1280, BF16, ST + 10240 + i * 2560) for i in range(2)]
    dnst = sb("dnst", 128, 2048, F32, ST)
    dnwb = sb("dnwb", 128, 4096, BF16, ST + 8192)
    b2f = sb("b2f", 8, 1024, F32, ST + 4096)
    xrb = [sb(f"xrb{i}", 128, 512, F32, ST + i * 2048) for i in range(2)]
    outb = [sb(f"outb{i}", 128, 512, F32, ST + 4096 + i * 2048) for i in range(2)]
    auto_ptr[0] = ST + 16384
    lnt = sb("lnt", 128, 512, F32)
    r_mean = sb("r_mean", 1, 512, F32, ST)
    r_aux = sb("r_aux", 1, 512, F32, ST + 2048)
    r_rstd = sb("r_rstd", 1, 512, BF16, ST + 4096)
    r_nmr = sb("r_nmr", 1, 512, BF16, ST + 5120)
    Xt = [sb(f"Xt{i}", 128, 128, F32) for i in range(2)]
    Yt = [sb(f"Yt{i}", 128, 128, F32) for i in range(2)]
    Rt = [sb(f"Rt{i}", 128, 256, F32) for i in range(2)]
    CBt = sb("CBt", 128, 128, F32)
    DIF = sb("DIF", 128, 128, F32)
    DPS = sb("DPS", 128, 128, F32)
    colsc = sb("colsc", 128, 4, F32)
    abt = sb("abt", 128, 2 * NT, F32)
    lat = sb("lat", 128, NT, F32)
    cumc = sb("cumc", 128, NT, F32)
    gnc = sb("gnc", 128, NT, F32)
    gmc = sb("gmc", 128, NT, F32)
    cnb = sb("cnb", 128, 1, F32)
    eye = sb("eye", 128, 128, F32)
    utones = sb("utones", 128, 128, F32)
    nbut = sb("nbut", 128, 128, F32)
    nblt = sb("nblt", 128, 128, F32)
    onesf = sb("onesf", 128, 128, F32)
    onesb = sb("onesb", 128, 512, BF16)
    causw = sb("causw", 128, 896, BF16)
    csel = sb("csel", 8, 1024, BF16)
    ccol = sb("ccol", 128, 8, F32)
    dnb = sb("dnb", 128, 4, F32)
    dnbo = sb("dnbo", 128, KC, F32)
    lnp = sb("lnp", 128, 4 * LAYERS * KC, F32)
    atbt = sb("atbt", 128, 3 * LAYERS * KC, F32)
    gatew = sb("gatew", 128, LAYERS * KC * E, BF16)
    gateb = sb("gateb", 1, LAYERS * E, BF16)
    b1b = [sb(f"b1b{i}", 128, NDFF, F32) for i in range(2)]
    b2t = sb("b2t", 8, 1024, BF16)
    lgtok = sb("lgtok", 128, NT * 8, F32)
    cwtok = sb("cwtok", 128, NT * 8, F32)
    tk = sb("tk", 128, 32, F32)
    cwcols = sb("cwcols", 128, 8, F32)
    cwsr = sb("cwsr", 1, 16, F32)
    cwst8 = sb("cwst8", 8, 2, F32)
    cwt_seq = sb("cwt_seq", 8, 512, BF16)
    hsum = sb("hsum", 128, KC, F32)
    hmean = sb("hmean", 128, KC, BF16)
    dnab = sb("dnab", 128, 1170, BF16)

    pp = [P.psum(f"pp{i}") for i in range(8)]

    def mm(psv, lhsT, rhs, start, stop, tp=None):
        return P.op("tensor",
                    (lambda e, o=psv.ap, a=lhsT.ap, b=rhs.ap, s1=start,
                     s2=stop, t=tp:
                     e.matmul(o, a, b, start=s1, stop=s2, tile_position=t)),
                    reads=[lhsT, rhs], writes=[psv])

    def tr(psv, inv, ident=None):
        idv = ident if ident is not None else eye.v
        return P.op("tensor",
                    (lambda e, o=psv.ap, a=inv.ap, i=idv.ap:
                     e.transpose(o, a, i)),
                    reads=[inv, idv], writes=[psv])

    def act(outv, inv, func, bias=0.0, scale=1.0):
        br = [bias] if isinstance(bias, View) else []
        bap = bias.ap if isinstance(bias, View) else bias
        return P.op("scalar",
                    (lambda e, o=outv.ap, i=inv.ap, f=func, b=bap, s=scale:
                     e.activation(o, i, f, bias=b, scale=s)),
                    reads=[inv] + br, writes=[outv])

    def vv(fname, outv, *ins, **kw):
        rd = [i for i in ins if isinstance(i, View)]
        aps = [i.ap if isinstance(i, View) else i for i in ins]
        return P.op("vector",
                    (lambda e, o=outv.ap, a=aps, f=fname, k=kw:
                     getattr(e, f)(o, *a, **k)),
                    reads=rd, writes=[outv])

    def ts2(outv, in0, s1, s2, op0, op1=None):
        rd = [in0] + [s for s in (s1, s2) if isinstance(s, View)]
        a1 = s1.ap if isinstance(s1, View) else s1
        a2 = s2.ap if isinstance(s2, View) else s2
        if op1 is None:
            return P.op("vector",
                        (lambda e, o=outv.ap, i=in0.ap, x=a1, y=a2, p0=op0:
                         e.tensor_scalar(o, i, x, y, p0)),
                        reads=rd, writes=[outv])
        return P.op("vector",
                    (lambda e, o=outv.ap, i=in0.ap, x=a1, y=a2, p0=op0, p1=op1:
                     e.tensor_scalar(o, i, x, y, p0, p1)),
                    reads=rd, writes=[outv])

    def stt(outv, in0, scal, in1, op0, op1):
        rd = [in0, in1] + ([scal] if isinstance(scal, View) else [])
        sap = scal.ap if isinstance(scal, View) else scal
        return P.op("vector",
                    (lambda e, o=outv.ap, i0=in0.ap, s=sap, i1=in1.ap, p0=op0,
                     p1=op1: e.scalar_tensor_tensor(o, i0, s, i1, p0, p1)),
                    reads=rd, writes=[outv])

    def conv(engname, outv, inv):
        if engname == "scalar":
            return act(outv, inv, AF.Identity)
        return P.op(engname,
                    (lambda e, o=outv.ap, i=inv.ap: e.tensor_copy(o, i)),
                    reads=[inv], writes=[outv])

    def memset(eng, outv, val):
        return P.op(eng, (lambda e, o=outv.ap, v=val: e.memset(o, v)),
                    reads=[], writes=[outv])

    for nm, tl in [("c_eye", eye), ("c_ut_ones", utones), ("c_nb_utinc", nbut),
                   ("c_nb_ltstrict", nblt), ("c_onesf", onesf),
                   ("c_onesb", onesb), ("c_causw", causw), ("c_csel", csel),
                   ("c_col", ccol), ("dn_bqkvg_r", dnb), ("dn_bo_fm", dnbo),
                   ("lnp_fm", lnp), ("atb_fm", atbt)]:
        P.dma("sync", tl.v, dram[nm], family="init")
    for kc in range(KC):
        P.dma("sync", xa[:, kc * T:(kc + 1) * T],
              dram["x_fm"][kc * 128:(kc + 1) * 128, :], family=f"xl{kc % 4}")
    P.dma("sync", dnst[:, 0:LAYERS * KC * E], dram["gate_w_fm"], family="init")
    P.dma("sync", dnst[0:1, 1024:1024 + LAYERS * E], dram["gate_b_row"],
          family="init")
    conv("vector", gatew.v, dnst[:, 0:LAYERS * KC * E])
    conv("vector", gateb.v, dnst[0:1, 1024:1024 + LAYERS * E])

    def _ckpt_init(ck):
        pass

    def lnw(which, l, kc):
        return lnp[:, (which * LAYERS + l) * KC + kc]

    def atb(which, l, kc):
        return atbt[:, (which * LAYERS + l) * KC + kc]

    def layer_norm(dst_tile, wb, pump_fn=None):
        for sl in range(NS):
            t0 = sl * 512
            if pump_fn is not None:
                pump_fn(2)
            pS = pp[6] if sl % 2 == 0 else pp[4]
            pQ = pp[7] if sl % 2 == 0 else pp[5]
            for kc in range(KC):
                i = (sl * KC + kc) % 4
                conv("vector", xbf[:, i * 512:(i + 1) * 512],
                     xa[:, kc * T + t0:kc * T + t0 + 512])
                act(xsq[:, i * 512:(i + 1) * 512],
                    xbf[:, i * 512:(i + 1) * 512], AF.Square)
                mm(pS[0:1, 0:512], onesb[:, 0:1],
                   xbf[:, i * 512:(i + 1) * 512], start=(kc == 0),
                   stop=(kc == KC - 1))
                mm(pQ[0:1, 0:512], onesb[:, 0:1],
                   xsq[:, i * 512:(i + 1) * 512], start=(kc == 0),
                   stop=(kc == KC - 1))
            act(r_mean.v, pS[0:1, 0:512], AF.Identity, scale=1.0 / D)
            act(r_aux.v, pQ[0:1, 0:512], AF.Identity, scale=1.0 / D)
            vv("tensor_mul", lnt[0:1, 0:512], r_mean.v, r_mean.v)
            vv("tensor_sub", r_aux.v, r_aux.v, lnt[0:1, 0:512])
            act(lnt[0:1, 0:512], r_aux.v, AF.Ln, bias=ccol[0:1, 0])
            act(r_rstd.v, lnt[0:1, 0:512], AF.Exp, scale=-0.5)
            act(lnt[0:1, 0:512], r_rstd.v, AF.Identity)
            vv("tensor_mul", lnt[0:1, 0:512], r_mean.v, lnt[0:1, 0:512])
            ts2(r_nmr.v, lnt[0:1, 0:512], -1.0, None, ALU.mult)
            mm(pS[0:128, 0:512], onesb[0:1, 0:128], r_rstd.v,
               start=True, stop=True)
            mm(pQ[0:128, 0:512], onesb[0:1, 0:128], r_nmr.v,
               start=True, stop=True)
            for kc in range(KC):
                vv("tensor_mul", lnt.v, xa[:, kc * T + t0:kc * T + t0 + 512],
                   pS[0:128, 0:512])
                dst = dst_tile[:, kc * T + t0:kc * T + t0 + 512]
                if wb is None:
                    vv("tensor_add", dst, lnt.v, pQ[0:128, 0:512])
                else:
                    l, which = wb
                    vv("tensor_add", lnt.v, lnt.v, pQ[0:128, 0:512])
                    ts2(dst, lnt.v, lnw(0 if which == 1 else 2, l, kc),
                        lnw(1 if which == 1 else 3, l, kc), ALU.mult, ALU.add)

    def ckpt(name):
        if STOP_AFTER == name:
            raise _StopBuild()

    def _build_body(ckpt):
        cv_ct = [0]

        def emit_conv(kind, lc, ec, ch):
            cv_ct[0] += 1
            i0 = cv_ct[0] % 2
            for hf in range(2):
                sbi = (2 * cv_ct[0] + hf) % 2
                if kind == "w1":
                    srcap = dram["e_w1_r"][lc, ec, ch][:, hf * 512:
                                                       hf * 512 + 512]
                else:
                    srcap = dram["e_w2"][lc, ec][ch * 128:(ch + 1) * 128,
                                                 hf * 512:hf * 512 + 512]
                P.dma("sync", w1f[sbi].v, srcap, family=f"cvf{sbi}")
                conv("gpsimd", cvb[i0][:, hf * 512:hf * 512 + 512],
                     w1f[sbi].v)
            dstd = w1bf_d if kind == "w1" else w2bf_d
            d = P.dma("sync", dstd[lc, ec, ch], cvb[i0].v,
                      family=f"cvs{i0}")
            P.dram_touch(d, f"{kind}d{lc}_{ec}_{ch}", True)

        XtS, YtS, RtS = [Xt, Xt2], [Yt, Yt2], [Rt, Rt2]
        CBtS, DIFS, DPSS = [CBt, CBt2], [DIF, DIF2], [DPS, DPS2]
        cumrS, cnbS, colscS = [cumr, cumr2], [cnb, cnb2], [colsc, colsc2]
        ckpt("init0")
        layer_norm(bfA, None)
        ckpt("lnA")

        for half in range(2):
            P.dma("sync", dnst.v,
                  dram["dn_wqkvg_r"][:, half * 2048:half * 2048 + 2048],
                  family="dnw")
            conv("vector", dnwb[:, half * 2048:half * 2048 + 1024],
                 dnst[:, 0:1024])
            conv("scalar", dnwb[:, half * 2048 + 1024:half * 2048 + 2048],
                 dnst[:, 1024:2048])
        P.dma("sync", dnst[:, 0:1170], dram["dn_vab_r"], family="dnw")
        conv("vector", dnab.v, dnst[:, 0:1170])

        def dnw_chunk(w, kc):
            return dnwb[:, w * 1024 + kc * 128:w * 1024 + kc * 128 + 128]

        for sl in range(NS):
            t0 = sl * 512
            for kcs in range(KC):
                mm(pp[0][0:128, 0:512], dnw_chunk(1, kcs),
                   bfA[:, kcs * T + t0:kcs * T + t0 + 512],
                   start=(kcs == 0), stop=(kcs == KC - 1))
            act(khat[:, t0:t0 + 512], pp[0][0:128, 0:512], AF.Identity, bias=dnb[:, 1])
            for tt2 in range(4):
                ttg = sl * 4 + tt2
                tb = ttg * 128
                for kcs in range(KC):
                    mm(pp[1][0:128, 0:130], bfA[:, kcs * T + tb:kcs * T + tb + 128],
                       dnab[:, kcs * 130:kcs * 130 + 130],
                       start=(kcs == 0), stop=False)
                mm(pp[1][0:128, 0:130], onesb[:, 0:128],
                   dnab[:, 8 * 130:8 * 130 + 130], start=False, stop=True)
                act(vtok[:, tb:tb + 128], pp[1][0:128, 0:128], AF.Identity)
                act(abt[:, ttg * 2:ttg * 2 + 2], pp[1][0:128, 128:130],
                    AF.Sigmoid)
        for sl in range(NS):
            t0 = sl * 512
            for kcs in range(KC):
                mm(pp[sl][0:128, 0:512], dnw_chunk(0, kcs),
                   bfA[:, kcs * T + t0:kcs * T + t0 + 512],
                   start=(kcs == 0), stop=(kcs == KC - 1))
        for sl in range(NS):
            t0 = sl * 512
            for kcs in range(KC):
                mm(pp[4 + sl][0:128, 0:512], dnw_chunk(3, kcs),
                   bfA[:, kcs * T + t0:kcs * T + t0 + 512],
                   start=(kcs == 0), stop=(kcs == KC - 1))
        for sl in range(NS):
            t0 = sl * 512
            act(qT[:, t0:t0 + 512], pp[sl][0:128, 0:512], AF.Identity, bias=dnb[:, 0])
            act(gTt[:, t0:t0 + 512], pp[4 + sl][0:128, 0:512], AF.Sigmoid,
                bias=dnb[:, 3])

        for sl in range(NS):
            t0 = sl * 512
            act(lnt.v, khat[:, t0:t0 + 512], AF.Square)
            mm(pp[0][0:1, 0:512], onesf[:, 0:1], lnt.v, start=True, stop=True)
            act(r_aux.v, pp[0][0:1, 0:512], AF.Ln)
            act(r_mean.v, r_aux.v, AF.Exp, scale=-0.5)
            ts2(r_mean.v, r_mean.v, ccol[0:1, 1], None, ALU.min)
            conv("vector", r_rstd.v, r_mean.v)
            mm(pp[0][0:128, 0:512], onesb[0:1, 0:128], r_rstd.v, start=True,
               stop=True)
            vv("tensor_mul", khat[:, t0:t0 + 512], khat[:, t0:t0 + 512],
               pp[0][0:128, 0:512])

        for cc in range(NT):
            bb = 4 if cc % 2 == 0 else 0
            cumrL, cnbL = cumrS[cc % 2], cnbS[cc % 2]
            act(lat[:, cc], abt[:, 2 * cc], AF.Ln)
            mm(pp[bb + 1][0:128, 0:1], utones.v, lat[:, cc], start=True, stop=True)
            conv("vector", cumc[:, cc], pp[bb + 1][0:128, 0:1])
            act(gmc[:, cc], cumc[:, cc], AF.Exp)
            mm(pp[bb + 1][0:1, 0:128], lat[:, cc], utones.v, start=True, stop=True)
            conv("vector", cumrL.v, pp[bb + 1][0:1, 0:128])
            mm(pp[bb + 2][0:128, 0:1], onesf[0:1, 0:128], cumrL[0:1, 127],
               start=True, stop=True)
            conv("vector", cnbL.v, pp[bb + 2][0:128, 0:1])
            P.op("scalar",
                 (lambda e, o=gnc[:, cc].ap, i=cumc[:, cc].ap, b=cnbL.v.ap:
                  e.activation(o, i, AF.Exp, bias=b, scale=-1.0)),
                 reads=[cumc[:, cc], cnbL.v], writes=[gnc[:, cc]])

        ckpt("proj")
        for cc in range(NT):
            cb = cc * 128
            bb = 4 if cc % 2 == 0 else 0
            XtL, YtL, RtL = XtS[cc % 2], YtS[cc % 2], RtS[cc % 2]
            CBtL, DIFL, DPSL = CBtS[cc % 2], DIFS[cc % 2], DPSS[cc % 2]
            cumrL, colscL = cumrS[cc % 2], colscS[cc % 2]
            ksl = khat[:, cb:cb + 128]
            tr(pp[bb][0:128, 0:128], ksl)
            conv("vector", ktok[:, cb:cb + 128], pp[bb][0:128, 0:128])
            mm(pp[bb + 1][0:128, 0:128], ksl, ksl, start=True, stop=True)
            mm(pp[bb + 2][0:1, 0:128], lat[:, cc], utones.v, start=True, stop=True)
            conv("vector", cumrL.v, pp[bb + 2][0:1, 0:128])
            mm(pp[bb + 2][0:128, 0:128], onesf[0:1, 0:128], cumrL.v, start=True,
               stop=True)
            ts2(CBtL.v, onesf.v, cumc[:, cc], None, ALU.mult)
            vv("tensor_sub", DIFL.v, CBtL.v, pp[bb + 2][0:128, 0:128])
            vv("tensor_add", DIFL.v, DIFL.v, nbut.v)
            act(DPSL.v, DIFL.v, AF.Exp)
            ts2(DPSL.v, DPSL.v, abt[:, 2 * cc + 1], -1.0, ALU.mult, ALU.mult)
            vv("tensor_mul", XtL[0].v, pp[bb + 1][0:128, 0:128], DPSL.v)
            tr(pp[bb + 1][0:128, 0:128], XtL[0].v)
            conv("vector", YtL[0].v, pp[bb + 1][0:128, 0:128])
            vv("tensor_mul", colscL[:, 0], abt[:, 2 * cc + 1], gmc[:, cc])
            ts2(RtL[0][:, 0:128], vtok[:, cb:cb + 128], abt[:, 2 * cc + 1], None,
                ALU.mult)
            ts2(RtL[0][:, 128:256], ktok[:, cb:cb + 128], colscL[:, 0], None,
                ALU.mult)
            cur = 0
            for j in range(7):
                mm(pp[bb + 2][0:128, 0:256], YtL[cur].v, RtL[cur].v, start=True,
                   stop=True)
                vv("tensor_add", RtL[1 - cur].v, RtL[cur].v,
                   pp[bb + 2][0:128, 0:256])
                if j < 6:
                    mm(pp[bb + 3][0:128, 0:128], YtL[cur].v, XtL[cur].v,
                       start=True, stop=True)
                    mm(pp[bb + 1][0:128, 0:128], XtL[cur].v, YtL[cur].v,
                       start=True, stop=True)
                    conv("vector", XtL[1 - cur].v, pp[bb + 3][0:128, 0:128])
                    conv("vector", YtL[1 - cur].v, pp[bb + 1][0:128, 0:128])
                cur = 1 - cur
            conv("vector", uv[:, cb:cb + 128], RtL[cur][:, 0:128])
            tr(pp[bb][0:128, 0:128], RtL[cur][:, 128:256])
            conv("vector", wt[:, cb:cb + 128], pp[bb][0:128, 0:128])
            ts2(DPSL.v, RtL[cur][:, 128:256], gnc[:, cc], -1.0, ALU.mult,
                ALU.mult)
            mm(pp[bb + 2][0:128, 0:128], DPSL.v, ktok[:, cb:cb + 128], start=True,
               stop=True)
            vv("tensor_mul", colscL[:, 1], gnc[:, cc], gmc[:, cc])
            stt(et[:, cb:cb + 128], eye.v, colscL[:, 1], pp[bb + 2][0:128, 0:128],
                ALU.mult, ALU.add)

        ckpt("ph1")
        memset("vector", hS[:, 0:128], 0.0)
        memset("vector", htS[:, 0:128], 0.0)
        for cc in range(NT - 1):
            cb = cc * 128
            bb = 4 if cc % 2 == 0 else 0
            XtL = XtS[cc % 2]
            ts2(XtL[0].v, uv[:, cb:cb + 128], gnc[:, cc], None, ALU.mult)
            ts2(XtL[1].v, ktok[:, cb:cb + 128], gnc[:, cc], None, ALU.mult)
            mm(pp[bb + 3][0:128, 0:128], et[:, cb:cb + 128], hS[:, cb:cb + 128],
               start=True, stop=False)
            mm(pp[bb + 3][0:128, 0:128], XtL[1].v, XtL[0].v, start=False,
               stop=True)
            conv("vector", hS[:, cb + 128:cb + 256], pp[bb + 3][0:128, 0:128])
            tr(pp[bb + 2][0:128, 0:128], hS[:, cb + 128:cb + 256])
            conv("vector", htS[:, cb + 128:cb + 256], pp[bb + 2][0:128, 0:128])

        ckpt("ph2")
        for cc in range(NT):
            cb = cc * 128
            bb = 4 if cc % 2 == 0 else 0
            XtL, YtL = XtS[cc % 2], YtS[cc % 2]
            CBtL, DIFL, DPSL = CBtS[cc % 2], DIFS[cc % 2], DPSS[cc % 2]
            cumrL = cumrS[cc % 2]
            tr(pp[bb][0:128, 0:128], uv[:, cb:cb + 128])
            conv("vector", XtL[0].v, pp[bb][0:128, 0:128])
            mm(pp[bb + 1][0:128, 0:128], hS[:, cb:cb + 128], wt[:, cb:cb + 128],
               start=True, stop=True)
            vv("tensor_sub", XtL[0].v, XtL[0].v, pp[bb + 1][0:128, 0:128])
            mm(pp[bb + 2][0:1, 0:128], lat[:, cc], utones.v, start=True, stop=True)
            conv("vector", cumrL.v, pp[bb + 2][0:1, 0:128])
            mm(pp[bb + 2][0:128, 0:128], onesf[0:1, 0:128], cumrL.v, start=True,
               stop=True)
            act(YtL[1].v, pp[bb + 2][0:128, 0:128], AF.Exp)
            vv("tensor_mul", YtL[0].v, qT[:, cb:cb + 128], YtL[1].v)
            ts2(CBtL.v, onesf.v, cumc[:, cc], None, ALU.mult)
            vv("tensor_sub", DIFL.v, pp[bb + 2][0:128, 0:128], CBtL.v)
            vv("tensor_add", DIFL.v, DIFL.v, nblt.v)
            act(DPSL.v, DIFL.v, AF.Exp)
            mm(pp[bb + 3][0:128, 0:128], XtL[0].v, qT[:, cb:cb + 128], start=True,
               stop=True)
            vv("tensor_mul", XtL[1].v, pp[bb + 3][0:128, 0:128], DPSL.v)
            mm(pp[bb][0:128, 0:128], htS[:, cb:cb + 128], YtL[0].v, start=True,
               stop=False)
            mm(pp[bb][0:128, 0:128], ktok[:, cb:cb + 128], XtL[1].v, start=False,
               stop=True)
            vv("tensor_mul", og[:, cb:cb + 128], pp[bb][0:128, 0:128],
               gTt[:, cb:cb + 128])

        ckpt("ph3")
        P.dma("sync", dnst[:, 0:1024], dram["dn_wo_r"], family="dnw")
        conv("vector", dnwb[:, 0:1024], dnst[:, 0:1024])
        for dc in range(KC):
            for sl in range(NS):
                t0 = sl * 512
                psv = pp[(dc * NS + sl) % 4]
                mm(psv[0:128, 0:512], dnwb[:, dc * 128:dc * 128 + 128],
                   og[:, t0:t0 + 512], start=True, stop=True)
                stt(xa[:, dc * T + t0:dc * T + t0 + 512], psv[0:128, 0:512],
                    dnbo[:, dc], xa[:, dc * T + t0:dc * T + t0 + 512],
                    ALU.add, ALU.add)

        ckpt("stageA")
        for kc in range(KC):
            d = P.dma("gpsimd", xres_d[kc * 128:(kc + 1) * 128, :],
                      xa[:, kc * T:(kc + 1) * T], family="spl")
            P.dram_touch(d, "xres", True)
        layer_norm(xa, None)
        ckpt("ln0")

        for l in range(LAYERS):
            layer_norm(bfA, (l, 1))
            for g in range(NG):
                for w_i, (wname, dst_t, bwhich) in enumerate(
                        [("at_wq_r", qt_g, 0), ("at_wk_r", kt_g, 1)]):
                    for half in range(2):
                        bi = (w_i * 2 + half) % 2
                        P.dma("sync", awf[bi][:, 0:1024],
                              dram[wname][l, g * 2 + half], family=f"aw{bi}")
                        conv("vector" if bi == 0 else "scalar",
                             awb[bi][:, 0:1024], awf[bi][:, 0:1024])
                        for sl in range(NS):
                            t0 = sl * 512
                            psv = pp[w_i * 2 + (sl % 2)]
                            for kcs in range(KC):
                                mm(psv[0:128, 0:512],
                                   awb[bi][:, kcs * 128:kcs * 128 + 128],
                                   bfA[:, kcs * T + t0:kcs * T + t0 + 512],
                                   start=(kcs == 0), stop=(kcs == KC - 1))
                            act(dst_t[:, half * T + t0:half * T + t0 + 512],
                                psv[0:128, 0:512], AF.Identity,
                                bias=atb(bwhich, l, g * 2 + half))
                for half in range(2):
                    P.dma("sync", awf[half][:, 0:1170],
                          dram["at_wv9_r"][l, g, half], family=f"aw{half}")
                    conv("vector" if half == 0 else "scalar",
                         awb[half][:, 0:1170], awf[half][:, 0:1170])
                for tt2 in range(NT):
                    tb = tt2 * 128
                    for half in range(2):
                        psv = pp[4 + half]
                        for kcs in range(KC):
                            mm(psv[0:128, 0:130],
                               bfA[:, kcs * T + tb:kcs * T + tb + 128],
                               awb[half][:, kcs * 130:kcs * 130 + 130],
                               start=(kcs == 0), stop=False)
                        mm(psv[0:128, 0:130], onesb[:, 0:128],
                           awb[half][:, 8 * 130:8 * 130 + 130],
                           start=False, stop=True)
                        act(vt_g[:, tt2 * 260 + half * 130:
                                 tt2 * 260 + half * 130 + 130],
                            psv[0:128, 0:130], AF.Identity)
                ckpt(f"qkv{l}_{g}")
                for cpair in range(2):
                    cT = cpair * T
                    for qs in range(NS):
                        q0 = qs * 512
                        kmax = 4 * qs + 3
                        hA, hB = 2 * cpair, 2 * cpair + 1

                        def emit_scores(kt):
                            kb = kt * 128
                            sA = pp[0] if kt % 2 == 0 else pp[5]
                            sB = pp[1] if kt % 2 == 0 else pp[6]
                            mm(sA[0:128, 0:512],
                               kt_g[0:64, cT + kb:cT + kb + 128],
                               qt_g[0:64, cT + q0:cT + q0 + 512],
                               start=True, stop=True, tp=(0, 0))
                            mm(sB[0:128, 0:512],
                               kt_g[64:128, cT + kb:cT + kb + 128],
                               qt_g[64:128, cT + q0:cT + q0 + 512],
                               start=True, stop=True, tp=(64, 0))

                        def emit_exp(kt):
                            sA = pp[0] if kt % 2 == 0 else pp[5]
                            sB = pp[1] if kt % 2 == 0 else pp[6]
                            act(pa[kt % 2].v, sA[0:128, 0:512], AF.Exp,
                                scale=0.125)
                            act(pb[kt % 2].v, sB[0:128, 0:512], AF.Exp,
                                scale=0.125)
                            m = kt - 4 * qs
                            if m >= 0:
                                s0 = 384 - 128 * m
                                vv("tensor_mul", pa[kt % 2].v, pa[kt % 2].v,
                                   causw[:, s0:s0 + 512])
                                vv("tensor_mul", pb[kt % 2].v, pb[kt % 2].v,
                                   causw[:, s0:s0 + 512])

                        def emit_o(kt):
                            mm(pp[2][0:65, 0:512],
                               vt_g[:, kt * 260 + 65 * hA:kt * 260 + 65 * hA + 65],
                               pa[kt % 2].v, start=(kt == 0), stop=(kt == kmax))
                            mm(pp[3][0:65, 0:512],
                               vt_g[:, kt * 260 + 65 * hB:kt * 260 + 65 * hB + 65],
                               pb[kt % 2].v, start=(kt == 0), stop=(kt == kmax))

                        emit_scores(0)
                        emit_exp(0)
                        for kt in range(kmax + 1):
                            if kt + 1 <= kmax:
                                emit_scores(kt + 1)
                                emit_exp(kt + 1)
                            emit_o(kt)
                        gchunk = 2 * g + cpair
                        for hh, ppo in ((0, pp[2]), (1, pp[3])):
                            act(r_aux[0:1, 0:512], ppo[64:65, 0:512], AF.Ln)
                            act(linv.v, r_aux[0:1, 0:512], AF.Exp, scale=-1.0)
                            mm(pp[4][0:64, 0:512], onesb[0:1, 0:64], linv.v,
                               start=True, stop=True)
                            conv("vector", linvb.v, pp[4][0:64, 0:512])
                            vv("tensor_mul",
                               og[hh * 64:hh * 64 + 64,
                                  gchunk * T + q0:gchunk * T + q0 + 512],
                               ppo[0:64, 0:512], linvb.v)
            ckpt(f"scores{l}")
            for rnd in range(2):
                for kk in range(4):
                    kcs = rnd * 4 + kk
                    bi = kk % 2
                    P.dma("sync", awf[bi][:, 0:1024],
                          dram["at_wo_r"][l][:, kcs, :], family=f"aw{bi}")
                    conv("vector" if bi == 0 else "scalar",
                         wobf[:, kk * 1024:kk * 1024 + 1024], awf[bi][:, 0:1024])
                for dc in range(KC):
                    for sl in range(NS):
                        t0 = sl * 512
                        psv = pp[5 + ((dc * NS + sl) % 2)]
                        for kk in range(4):
                            kcs = rnd * 4 + kk
                            mm(psv[0:128, 0:512],
                               wobf[:, kk * 1024 + dc * 128:
                                    kk * 1024 + dc * 128 + 128],
                               og[:, kcs * T + t0:kcs * T + t0 + 512],
                               start=(kk == 0), stop=(kk == 3))
                        if rnd == 0:
                            stt(xa[:, dc * T + t0:dc * T + t0 + 512],
                                psv[0:128, 0:512], atb(2, l, dc),
                                xa[:, dc * T + t0:dc * T + t0 + 512],
                                ALU.add, ALU.add)
                        else:
                            stt(xa[:, dc * T + t0:dc * T + t0 + 512],
                                psv[0:128, 0:512], ccol[:, 3],
                                xa[:, dc * T + t0:dc * T + t0 + 512],
                                ALU.mult, ALU.add)

            ckpt(f"attn{l}")
            layer_norm(bfA, (l, 2))
            P.dma("sync", b2f.v, dram["e_b2"][l], family="b2l")
            conv("vector", b2t.v, b2f.v)
            if l % 2 == 0:
                for sl in range(NS):
                    t0 = sl * 512
                    for kcs in range(KC):
                        mm(pp[5][0:8, 0:512],
                           gatew[:, (l * KC + kcs) * E:(l * KC + kcs) * E + 8],
                           bfA[:, kcs * T + t0:kcs * T + t0 + 512],
                           start=(kcs == 0), stop=False)
                    mm(pp[5][0:8, 0:512], gateb[0:1, l * E:l * E + 8],
                       onesb[0:1, 0:512], start=False, stop=True)
                    act(lgT[0:8, t0:t0 + 512], pp[5][0:8, 0:512], AF.Identity)
                ckpt(f"lgT{l}")
                for tt2 in range(NT):
                    tb = tt2 * 128
                    tr(pp[6][0:128, 0:8], lgT[0:8, tb:tb + 128],
                       ident=eye[0:8, 0:8])
                    conv("vector", lgtok[:, tt2 * 8:tt2 * 8 + 8],
                         pp[6][0:128, 0:8])
                    lg8 = lgtok[:, tt2 * 8:tt2 * 8 + 8]
                    vv("reduce_max", tk[:, 0], lg8, axis=AX.X)
                    ts2(tk[:, 8:16], lg8, tk[:, 0], None, ALU.is_equal)
                    stt(tk[:, 16:24], tk[:, 8:16], ccol[:, 2], lg8, ALU.mult,
                        ALU.add)
                    vv("reduce_max", tk[:, 1], tk[:, 16:24], axis=AX.X)
                    ts2(tk[:, 24:32], tk[:, 16:24], tk[:, 1], None, ALU.is_equal)
                    vv("tensor_sub", tk[:, 2], tk[:, 1], tk[:, 0])
                    act(tk[:, 3], tk[:, 2], AF.Exp)
                    ts2(tk[:, 4], tk[:, 3], 1.0, None, ALU.add)
                    vv("reciprocal", tk[:, 4], tk[:, 4])
                    vv("tensor_mul", tk[:, 5], tk[:, 3], tk[:, 4])
                    ts2(cwtok[:, tt2 * 8:tt2 * 8 + 8], tk[:, 8:16], tk[:, 4],
                        None, ALU.mult)
                    stt(cwtok[:, tt2 * 8:tt2 * 8 + 8], tk[:, 24:32], tk[:, 5],
                        cwtok[:, tt2 * 8:tt2 * 8 + 8], ALU.mult, ALU.add)
                    tr(pp[6][0:8, 0:128], cwtok[:, tt2 * 8:tt2 * 8 + 8])
                    conv("vector", cwT[0:8, tb:tb + 128], pp[6][0:8, 0:128])
            else:
                for kcs in range(KC):
                    vv("reduce_sum", hsum[:, kcs], bfA[:, kcs * T:(kcs + 1) * T],
                       axis=AX.X)
                ts2(hmean.v, hsum.v, 1.0 / T, None, ALU.mult)
                for kcs in range(KC):
                    mm(pp[5][0:8, 0:1],
                       gatew[:, (l * KC + kcs) * E:(l * KC + kcs) * E + 8],
                       hmean[:, kcs], start=(kcs == 0), stop=False)
                mm(pp[5][0:8, 0:1], gateb[0:1, l * E:l * E + 8], onesb[0:1, 0:1],
                   start=False, stop=True)
                conv("vector", cwst8[0:8, 0], pp[5][0:8, 0:1])
                tr(pp[6][0:1, 0:8], cwst8[0:8, 0], ident=eye[0:8, 0:8])
                conv("vector", cwsr[0:1, 0:8], pp[6][0:1, 0:8])
                lg8 = cwsr[0:1, 0:8]
                vv("reduce_max", cwsr[0:1, 8], lg8, axis=AX.X)
                ts2(tk[0:1, 8:16], lg8, cwsr[0:1, 8], None, ALU.is_equal)
                stt(tk[0:1, 16:24], tk[0:1, 8:16], ccol[0:1, 2], lg8, ALU.mult,
                    ALU.add)
                vv("reduce_max", cwsr[0:1, 9], tk[0:1, 16:24], axis=AX.X)
                ts2(tk[0:1, 24:32], tk[0:1, 16:24], cwsr[0:1, 9], None,
                    ALU.is_equal)
                vv("tensor_sub", cwsr[0:1, 10], cwsr[0:1, 9], cwsr[0:1, 8])
                act(cwsr[0:1, 11], cwsr[0:1, 10], AF.Exp)
                ts2(cwsr[0:1, 12], cwsr[0:1, 11], 1.0, None, ALU.add)
                vv("reciprocal", cwsr[0:1, 12], cwsr[0:1, 12])
                vv("tensor_mul", cwsr[0:1, 13], cwsr[0:1, 11], cwsr[0:1, 12])
                ts2(tk[0:1, 0:8], tk[0:1, 8:16], cwsr[0:1, 12], None, ALU.mult)
                stt(tk[0:1, 0:8], tk[0:1, 24:32], cwsr[0:1, 13], tk[0:1, 0:8],
                    ALU.mult, ALU.add)
                for e in range(E):
                    mm(pp[6][0:128, 0:1], onesf[0:1, 0:128], tk[0:1, e],
                       start=True, stop=True)
                    conv("vector", cwcols[:, e], pp[6][0:128, 0:1])
                tr(pp[6][0:8, 0:1], tk[0:1, 0:8], ident=eye[0:1, 0:1])
                conv("vector", cwst8[0:8, 1], pp[6][0:8, 0:1])
                ts2(cwt_seq.v, onesb[0:8, 0:512], cwst8[0:8, 1], None, ALU.mult)

            ckpt(f"gate{l}")
            for ch in range(NDFF):
                emit_conv("w1", l, 0, ch)
            for p_i in range(NP):
                pt0 = p_i * TS
                for e in range(E):
                    if l % 2 == 0:
                        for s2 in range(NS2):
                            mm(pp[5][0:128, 0:512],
                               csel[0:8, e * 128:e * 128 + 128],
                               cwT[0:8, pt0 + s2 * 512:pt0 + s2 * 512 + 512],
                               start=True, stop=True)
                            act(cwbT[:, s2 * 512:s2 * 512 + 512],
                                pp[5][0:128, 0:512], AF.Identity)
                    P.dma("sync", b1b[e % 2].v, dram["e_b1_fm"][l, e],
                          family=f"b1{e % 2}")
                    for df in range(NDFF):
                        buf = df % 2
                        if p_i == 0:
                            emit_conv("w2", l, e, df)
                        d = P.dma("scalar", w1b[buf].v, w1bf_d[l, e, df],
                                  family=f"w1r{buf}")
                        P.dram_touch(d, f"w1d{l}_{e}_{df}", False)
                        for s2 in range(NS2):
                            t0 = pt0 + s2 * 512
                            psv = pp[(df * NS2 + s2) % 4]
                            for kcs in range(KC):
                                mm(psv[0:128, 0:512],
                                   w1b[buf][:, kcs * 128:kcs * 128 + 128],
                                   bfA[:, kcs * T + t0:kcs * T + t0 + 512],
                                   start=(kcs == 0), stop=(kcs == KC - 1))
                            act(gelt[s2 % 2].v, psv[0:128, 0:512], AF.Gelu,
                                bias=b1b[e % 2][:, df])
                            h1d = h1p[:, df * TS + s2 * 512:df * TS + s2 * 512 + 512]
                            if l % 2 == 0:
                                vv("tensor_mul", h1d, gelt[s2 % 2].v,
                                   cwbT[:, s2 * 512:s2 * 512 + 512])
                            else:
                                ts2(h1d, gelt[s2 % 2].v, cwcols[:, e], None,
                                    ALU.mult)
                    for s2 in range(NS2):
                        t0 = pt0 + s2 * 512
                        cwsrc = (cwT[0:8, t0:t0 + 512] if l % 2 == 0
                                 else cwt_seq[0:8, 0:512])
                        if e == 0:
                            for dc in range(KC):
                                mm(pp[dc][0:128, 0:512],
                                   b2t[0:8, dc * 128:dc * 128 + 128], cwsrc,
                                   start=True, stop=False)
                        for k in range(NDFF):
                            buf = k % 4
                            if p_i == 0 and s2 == 0 and e < E - 1:
                                emit_conv("w1", l, e + 1, k)
                            d = P.dma("scalar", w2b[buf].v, w2bf_d[l, e, k],
                                      family=f"w2r{buf}")
                            P.dram_touch(d, f"w2d{l}_{e}_{k}", False)
                            for dc in range(KC):
                                mm(pp[dc][0:128, 0:512],
                                   w2b[buf][:, dc * 128:dc * 128 + 128],
                                   h1p[:, k * TS + s2 * 512:k * TS + s2 * 512 + 512],
                                   start=(k == 0 and e != 0), stop=(k == NDFF - 1))
                        for dc in range(KC):
                            stt(xa[:, dc * T + t0:dc * T + t0 + 512],
                                pp[dc][0:128, 0:512], ccol[:, 3],
                                xa[:, dc * T + t0:dc * T + t0 + 512],
                                ALU.mult, ALU.add)

        ckpt("moe")
        for kc in range(KC):
            for sl in range(NS):
                t0 = sl * 512
                i = (kc * NS + sl) % 2
                d = P.dma("sync", xrb[i].v,
                          xres_d[kc * 128:(kc + 1) * 128, t0:t0 + 512],
                          family=f"rld{i}")
                P.dram_touch(d, "xres", False)
                vv("tensor_add", outb[i].v, xrb[i].v,
                   xa[:, kc * T + t0:kc * T + t0 + 512])
                P.dma("gpsimd", out_fm[kc * 128:(kc + 1) * 128, t0:t0 + 512],
                      outb[i].v, family="out")

    try:
        _build_body(ckpt)
    except _StopBuild:
        pass

    for tapname, tname, p0, p1, f0, f1 in TAP_SPEC:
        tl = tiles[tname]
        tout = nc.dram_tensor(tapname, [p1 - p0, f1 - f0],
                              tl.dtype, kind="ExternalOutput").ap()
        P.dma("gpsimd", tout, tl[p0:p1, f0:f1], family="out")

    n_ops = P.build(final_wait_families=["out0", "out1", "out2", "out3", "out"])
    return nc, n_ops


def _prep_shared(i):
    f32 = np.float32
    bf = np.dtype("bfloat16") if False else None
    d = {}
    Wq, Wk, Wv, Wg = (np.asarray(i["L0_attn_Wq"]), np.asarray(i["L0_attn_Wk"]),
                      np.asarray(i["L0_attn_Wv"]), np.asarray(i["L0_attn_Wg"]))
    wqkvg = np.stack([Wq, Wk, Wv, Wg])
    d["dn_wqkvg_r"] = np.ascontiguousarray(
        wqkvg.reshape(4, KC, 128, 128).transpose(2, 0, 1, 3)
        .reshape(128, 4 * KC * 128)).astype(f32)
    d["dn_bqkvg_r"] = np.ascontiguousarray(np.stack(
        [np.asarray(i["L0_attn_bq"]), np.asarray(i["L0_attn_bk"]),
         np.asarray(i["L0_attn_bv"]), np.asarray(i["L0_attn_bg"])], axis=1)
    ).astype(f32)
    vab9 = np.zeros((9, 128, 130), f32)
    vab9[0:8, :, 0:128] = np.asarray(i["L0_attn_Wv"]).reshape(8, 128, 128)
    vab9[8, 0, 0:128] = np.asarray(i["L0_attn_bv"])
    vab9[0:8, :, 128:130] = np.concatenate(
        [np.asarray(i["L0_attn_Wa"]), np.asarray(i["L0_attn_Wb"])],
        axis=1).reshape(8, 128, 2)
    vab9[8, 0, 128] = np.asarray(i["L0_attn_ba"])[0]
    vab9[8, 0, 129] = np.asarray(i["L0_attn_bb"])[0]
    d["dn_vab_r"] = np.ascontiguousarray(
        vab9.transpose(1, 0, 2).reshape(128, 9 * 130)).astype(f32)
    d["dn_wo_r"] = np.ascontiguousarray(np.asarray(i["L0_attn_Wo"])).astype(f32)
    d["dn_bo_fm"] = np.ascontiguousarray(
        np.asarray(i["L0_attn_bo"]).reshape(KC, 128).T).astype(f32)
    wq = np.asarray(i["L0_ffn_wq"])
    wk = np.asarray(i["L0_ffn_wk"])
    wv = np.asarray(i["L0_ffn_wv"])
    wo = np.asarray(i["L0_ffn_wo"])
    d["at_wq_r"] = np.ascontiguousarray(
        wq.reshape(LAYERS, KC, 128, KC, 128).transpose(0, 3, 2, 1, 4)
        .reshape(LAYERS, KC, 128, KC * 128)).astype(f32)
    d["at_wk_r"] = np.ascontiguousarray(
        wk.reshape(LAYERS, KC, 128, KC, 128).transpose(0, 3, 2, 1, 4)
        .reshape(LAYERS, KC, 128, KC * 128)).astype(f32)
    bv = np.asarray(i["L0_ffn_bv"])
    wv9 = np.zeros((LAYERS, NG, 2, 9, 128, 130), f32)
    for l in range(LAYERS):
        for g in range(NG):
            for half in range(2):
                for hloc in range(2):
                    h = 4 * g + half * 2 + hloc
                    cols = slice(hloc * 65, hloc * 65 + 64)
                    wcol = wv[l][:, h * 64:(h + 1) * 64]
                    wv9[l, g, half, 0:8, :, cols] = wcol.reshape(KC, 128, 64)
                    wv9[l, g, half, 8, 0, cols] = bv[l][h * 64:(h + 1) * 64]
                    wv9[l, g, half, 8, 0, hloc * 65 + 64] = 1.0
    d["at_wv9_r"] = np.ascontiguousarray(
        wv9.transpose(0, 1, 2, 4, 3, 5).reshape(LAYERS, NG, 2, 128, 9 * 130)
    ).astype(f32)
    d["at_wo_r"] = np.ascontiguousarray(
        wo.reshape(LAYERS, KC, 128, D).transpose(0, 2, 1, 3)).astype(f32)
    fm = lambda a: np.ascontiguousarray(
        np.asarray(a).reshape(LAYERS, KC, 128).transpose(2, 0, 1)
        .reshape(128, LAYERS * KC))
    d["atb_fm"] = np.ascontiguousarray(np.concatenate(
        [fm(i["L0_ffn_bq"]), fm(i["L0_ffn_bk"]), fm(i["L0_ffn_bo"])],
        axis=1)).astype(f32)
    d["lnp_fm"] = np.ascontiguousarray(np.concatenate(
        [fm(i["L0_ffn_ln1_w"]), fm(i["L0_ffn_ln1_b"]),
         fm(i["L0_ffn_ln2_w"]), fm(i["L0_ffn_ln2_b"])], axis=1)).astype(f32)
    gw = np.asarray(i["L0_ffn_gate_w"])
    d["gate_w_fm"] = np.ascontiguousarray(
        gw.reshape(LAYERS, KC, 128, E).transpose(2, 0, 1, 3)
        .reshape(128, LAYERS * KC * E)).astype(f32)
    d["gate_b_row"] = np.ascontiguousarray(
        np.asarray(i["L0_ffn_gate_b"]).reshape(1, LAYERS * E)).astype(f32)
    ew1 = np.asarray(i["L0_ffn_e_w1"])
    d["e_w1_r"] = np.ascontiguousarray(
        ew1.reshape(LAYERS, E, KC, 128, NDFF, 128).transpose(0, 1, 4, 3, 2, 5)
        .reshape(LAYERS, E, NDFF, 128, KC * 128)).astype(f32)
    eb1 = np.asarray(i["L0_ffn_e_b1"])
    d["e_b1_fm"] = np.ascontiguousarray(
        eb1.reshape(LAYERS, E, NDFF, 128).transpose(0, 1, 3, 2)).astype(f32)
    d["e_w2"] = np.ascontiguousarray(np.asarray(i["L0_ffn_e_w2"])).astype(f32)
    d["e_b2"] = np.ascontiguousarray(np.asarray(i["L0_ffn_e_b2"])).astype(f32)
    d["c_eye"] = np.eye(128, dtype=f32)
    d["c_ut_ones"] = np.triu(np.ones((128, 128), f32))
    d["c_nb_utinc"] = np.where(np.triu(np.ones((128, 128), bool)),
                               f32(NEGBIG), f32(0.0)).astype(f32)
    d["c_nb_ltstrict"] = np.where(np.tril(np.ones((128, 128), bool), -1),
                                  f32(NEGBIG), f32(0.0)).astype(f32)
    d["c_onesf"] = np.ones((128, 128), f32)
    import ml_dtypes
    bf16 = ml_dtypes.bfloat16
    d["c_onesb"] = np.ones((128, 512), bf16)
    rr = np.arange(128)[:, None]
    cc = np.arange(896)[None, :]
    d["c_causw"] = (rr <= (cc - 384)).astype(bf16)
    cs = np.zeros((8, 1024), np.float32)
    for e in range(E):
        cs[e, e * 128:(e + 1) * 128] = 1.0
    d["c_csel"] = cs.astype(bf16)
    col = np.zeros((128, 8), f32)
    col[:, 0] = 1e-5
    col[:, 1] = 1e12
    col[:, 2] = NEGBIG
    col[:, 3] = 1.0
    d["c_col"] = col
    return d


_CACHED = {}


def kernel(**inputs):
    T = T_FULL
    x = np.asarray(inputs["L0_attn_x"])
    B = x.shape[0]
    shared = _prep_shared(inputs)
    nc, n_ops = build_program(T)
    in_maps = []
    for b in range(B):
        m = dict(shared)
        m["x_fm"] = np.ascontiguousarray(x[b].T.astype(np.float32))
        in_maps.append(m)
    res = run_bass_kernel_spmd(nc, in_maps, core_ids=list(range(B)))
    out = np.stack([np.ascontiguousarray(r["out_fm"].T)
                    for r in res.results], axis=0)
    return out.astype(np.float32)


# version 15
# speedup vs baseline: 1.4030x; 1.4030x over previous
import numpy as np
import concourse.bass as bass
import concourse.mybir as mybir
from concourse.bass_utils import run_bass_kernel_spmd

F32 = mybir.dt.float32
BF16 = mybir.dt.bfloat16
AF = mybir.ActivationFunctionType
ALU = mybir.AluOpType
AX = mybir.AxisListType

D = 1024
KC = 8
HEADS = 16
HD = 64
NG = 4
E = 8
DFF = 4096
NDFF = 32
LAYERS = 2
NEGBIG = -30000.0
SB_CAP = 212000
N_CORES = 8
T_FULL = 2048

TAP_SPEC = []
STOP_AFTER = None


class _StopBuild(Exception):
    pass


class View:
    __slots__ = ("ap", "space", "p0", "p1", "b0", "b1")

    def __init__(self, ap, space, p0, p1, b0, b1):
        self.ap, self.space = ap, space
        self.p0, self.p1, self.b0, self.b1 = p0, p1, b0, b1


class Tile:
    def __init__(self, handle, rows, cols, dtype, offset, space):
        self.h, self.rows, self.cols = handle, rows, cols
        self.dtype = dtype
        self.esz = 2 if dtype == BF16 else 4
        self.off, self.space = offset, space

    def __getitem__(self, key):
        ps, fs = key
        if isinstance(ps, slice):
            p0 = ps.start or 0
            p1 = ps.stop if ps.stop is not None else self.rows
        else:
            p0, p1 = ps, ps + 1
        if isinstance(fs, slice):
            f0 = fs.start or 0
            f1 = fs.stop if fs.stop is not None else self.cols
        else:
            f0, f1 = fs, fs + 1
        return View(self.h[p0:p1, f0:f1], self.space, p0, p1,
                    self.off + f0 * self.esz, self.off + f1 * self.esz)

    @property
    def v(self):
        return self[:, :]


class Op:
    __slots__ = ("idx", "eng", "fn", "reads", "writes", "is_dma", "family",
                 "inc_needed", "edges")

    def __init__(self, idx, eng, fn, reads, writes, is_dma, family):
        self.idx, self.eng, self.fn = idx, eng, fn
        self.reads, self.writes = reads, writes
        self.is_dma, self.family = is_dma, family
        self.inc_needed = False
        self.edges = []


class Prog:
    ENGINES = ("sync", "tensor", "vector", "scalar", "gpsimd")

    def __init__(self, nc):
        self.nc = nc
        self.ops = []
        self.mem = {}
        self.last_fam = {}

    def psum(self, name):
        h = self.nc.alloc_psum_tensor(name, [128, 512], F32)
        return Tile(h, 128, 512, F32, 0, "PS_" + name)

    @staticmethod
    def _ovl(a, b):
        return not (a.p1 <= b.p0 or b.p1 <= a.p0 or a.b1 <= b.b0 or b.b1 <= a.b0)

    def _record(self, op):
        deps = {}
        for r in op.reads:
            for rect, p, is_w in self.mem.get(r.space, ()):
                if is_w and self._ovl(r, rect):
                    deps[p.idx] = p
        for w in op.writes:
            for rect, p, is_w in self.mem.get(w.space, ()):
                if self._ovl(w, rect):
                    if is_w or p.eng != op.eng or p.is_dma or op.is_dma:
                        deps[p.idx] = p
        for idx in sorted(deps):
            p = deps[idx]
            if (p.eng == "tensor" and op.eng == "tensor"
                    and not p.is_dma and not op.is_dma):
                continue
            op.edges.append(p)
        for w in op.writes:
            lst = self.mem.setdefault(w.space, [])
            lst[:] = [rec for rec in lst
                      if not (w.p0 <= rec[0].p0 and rec[0].p1 <= w.p1
                              and w.b0 <= rec[0].b0 and rec[0].b1 <= w.b1)]
            lst.append((w, op, True))
        for r in op.reads:
            self.mem.setdefault(r.space, []).append((r, op, False))

    def op(self, eng, fn, reads=(), writes=()):
        o = Op(len(self.ops), eng, fn, list(reads), list(writes), False, None)
        self.ops.append(o)
        self._record(o)
        return o

    def dma(self, eng, out, in_, family, **kw):
        out_ap = out.ap if isinstance(out, View) else out
        in_ap = in_.ap if isinstance(in_, View) else in_
        reads = [in_] if isinstance(in_, View) else []
        writes = [out] if isinstance(out, View) else []
        fn = (lambda e, o=out_ap, i=in_ap, k=kw: e.dma_start(out=o, in_=i, **k))
        o = Op(len(self.ops), eng, fn, reads, writes, True, family)
        self.ops.append(o)
        self._record(o)
        prev = self.last_fam.get(family)
        if prev is not None:
            o.edges.append(prev)
        self.last_fam[family] = o
        return o

    def dram_touch(self, op_obj, space, write):
        rect = View(None, space, 0, 1 << 20, 0, 1 << 40)
        for r0, p, is_w in list(self.mem.get(space, ())):
            if (is_w or write) and p is not op_obj:
                op_obj.edges.append(p)
        self.mem.setdefault(space, []).append((rect, op_obj, write))

    def build(self, final_wait_families):
        nc = self.nc
        for o in self.ops:
            for p in o.edges:
                p.inc_needed = True
        eng_counts = {e: 0 for e in self.ENGINES}
        fam_counts = {}
        opval = {}
        for o in self.ops:
            if o.is_dma:
                fam_counts[o.family] = fam_counts.get(o.family, 0) + 1
                opval[o.idx] = ("fam", o.family, fam_counts[o.family] * 16)
            elif o.inc_needed:
                eng_counts[o.eng] += 1
                opval[o.idx] = ("eng", o.eng, eng_counts[o.eng])
        eng_sems = {e: nc.alloc_semaphore(name=f"ch_{e}")
                    for e in self.ENGINES if eng_counts[e]}
        fam_sems = {f: nc.alloc_semaphore(name=f"fm_{f}") for f in fam_counts}
        streams = {e: [] for e in self.ENGINES}
        for o in self.ops:
            streams[o.eng].append(o)

        def emitter(engname):
            def run(eng):
                wm = {}
                for o in streams[engname]:
                    best = {}
                    for p in o.edges:
                        if p.idx not in opval:
                            continue
                        kind, key, val = opval[p.idx]
                        if best.get((kind, key), -1) < val:
                            best[(kind, key)] = val
                    for (kind, key), val in sorted(best.items(),
                                                   key=lambda kv: str(kv[0])):
                        if wm.get((kind, key), -1) >= val:
                            continue
                        sem = fam_sems[key] if kind == "fam" else eng_sems[key]
                        eng.wait_ge(sem, val)
                        wm[(kind, key)] = val
                    inst = o.fn(eng)
                    if o.idx in opval:
                        kind, key, _ = opval[o.idx]
                        sem = fam_sems[key] if kind == "fam" else eng_sems[key]
                        inst.then_inc(sem, 16 if kind == "fam" else 1)
                if engname == "gpsimd":
                    for fam in final_wait_families:
                        if fam in fam_sems:
                            eng.wait_ge(fam_sems[fam], fam_counts[fam] * 16)
            return run

        with nc.allow_low_precision(reason="bf16 compute sanctioned by "
                                      "problem tolerance"), nc.Block() as blk:
            blk.sync(emitter("sync"))
            blk.tensor(emitter("tensor"))
            blk.vector(emitter("vector"))
            blk.scalar(emitter("scalar"))
            blk.gpsimd(emitter("gpsimd"))
        return len(self.ops)


def build_program(T):
    NT = T // 128
    NS = T // 512
    TS = min(1024, T)
    NP = T // TS
    NS2 = TS // 512

    nc = bass.Bass("TRN2", target_bir_lowering=False)
    _probe = nc.alloc_sbuf_tensor("_base_probe", [128, 8], F32)
    SB_BASE = nc.lookup_mloc(_probe).addr
    SB_TOP = SB_BASE + nc.sbuf_bytes_remaining + 32
    P = Prog(nc)
    dram = {}

    def din(name, shape, dtype=F32):
        dram[name] = nc.dram_tensor(name, list(shape), dtype,
                                    kind="ExternalInput").ap()
        return dram[name]

    din("x_fm", (D, T))
    out_fm = nc.dram_tensor("out_fm", [D, T], F32, kind="ExternalOutput").ap()
    din("dn_wqkvg_r", (128, 4 * KC * 128))
    din("dn_bqkvg_r", (128, 4))
    din("dn_vab_r", (128, 9 * 130))
    din("dn_wo_r", (128, KC * 128))
    din("dn_bo_fm", (128, KC))
    din("at_wq_r", (LAYERS, KC, 128, KC * 128))
    din("at_wk_r", (LAYERS, KC, 128, KC * 128))
    din("at_wv9_r", (LAYERS, NG, 2, 128, 9 * 130))
    din("at_wo_r", (LAYERS, 128, KC, 1024))
    din("atb_fm", (128, 3 * LAYERS * KC))
    din("lnp_fm", (128, 4 * LAYERS * KC))
    din("gate_w_fm", (128, LAYERS * KC * E))
    din("gate_b_row", (1, LAYERS * E))
    din("e_w1_r", (LAYERS, E, NDFF, 128, KC * 128))
    din("e_b1_fm", (LAYERS, E, 128, NDFF))
    din("e_w2", (LAYERS, E, DFF, D))
    din("e_b2", (LAYERS, E, D))
    din("c_eye", (128, 128))
    din("c_ut_ones", (128, 128))
    din("c_nb_utinc", (128, 128))
    din("c_nb_ltstrict", (128, 128))
    din("c_onesf", (128, 128))
    din("c_onesb", (128, 512), BF16)
    din("c_causw", (128, 896), BF16)
    din("c_csel", (8, 1024), BF16)
    din("c_col", (128, 8))

    xres_d = nc.dram_tensor("xres_d", [D, T], F32, kind="Internal").ap()
    w1bf_d = nc.dram_tensor("w1bf_d", [LAYERS, E, NDFF, 128, 1024], BF16,
                            kind="Internal").ap()
    w2bf_d = nc.dram_tensor("w2bf_d", [LAYERS, E, NDFF, 128, 1024], BF16,
                            kind="Internal").ap()

    tiles = {}
    auto_ptr = [0]

    def sb(name, rows, cols, dtype, offset=None):
        esz = 2 if dtype == BF16 else 4
        if offset is None:
            offset = (auto_ptr[0] + 31) & ~31
            auto_ptr[0] = offset + cols * esz
        assert offset + cols * esz + SB_BASE <= SB_TOP, (name, offset + cols * esz)
        h = nc.alloc_sbuf_tensor_at(name, [rows, cols], dtype,
                                    offset=offset + SB_BASE)
        t = Tile(h, rows, cols, dtype, offset, "SB")
        tiles[name] = t
        return t

    xa = sb("xa", 128, KC * T, F32, 0)
    BF0 = KC * T * 4
    bfA = sb("bfA", 128, KC * T, BF16, BF0)
    qT = sb("qT", 128, T, F32, BF0)
    gTt = sb("gTt", 128, T, F32, BF0 + T * 4)
    AR = BF0 + KC * T * 2
    ktok = sb("ktok", 128, T, F32, AR)
    uv = sb("uv", 128, T, F32, AR + T * 4)
    wt = sb("wt", 128, T, F32, AR + 2 * T * 4)
    et = sb("et", 128, T, F32, AR + 3 * T * 4)
    hS = sb("hS", 128, T, F32, AR + 4 * T * 4)
    htS = sb("htS", 128, T, F32, AR + 5 * T * 4)
    khat = sb("khat", 128, T, F32, AR + 6 * T * 4)
    vtok = sb("vtok", 128, T, F32, AR + 7 * T * 4)
    qt_g = sb("qt_g", 128, 2 * T, BF16, AR)
    kt_g = sb("kt_g", 128, 2 * T, BF16, AR + 4 * T)
    vt_g = sb("vt_g", 128, NT * 260, BF16, AR + 8 * T)
    og = sb("og", 128, KC * T, BF16, AR + 8 * T + ((NT * 260 * 2 + 31) & ~31))
    h1p = sb("h1p", 128, NDFF * TS, BF16, AR)
    SHR = AR + max(8 * T * 4, NDFF * TS * 2)
    xbf = sb("xbf", 128, 2048, BF16, SHR)
    xsq = sb("xsq", 128, 2048, BF16, SHR + 4096)
    cumr = sb("cumr", 1, 128, F32, SHR)
    Xt2 = [sb(f"Xt2_{i}", 128, 128, F32, SHR + 1024 + i * 512) for i in range(2)]
    Yt2 = [sb(f"Yt2_{i}", 128, 128, F32, SHR + 2048 + i * 512) for i in range(2)]
    Rt2 = [sb(f"Rt2_{i}", 128, 256, F32, SHR + 3072 + i * 1024) for i in range(2)]
    CBt2 = sb("CBt2", 128, 128, F32, SHR + 5120)
    DIF2 = sb("DIF2", 128, 128, F32, SHR + 5632)
    DPS2 = sb("DPS2", 128, 128, F32, SHR + 6144)
    cumr2 = sb("cumr2", 1, 128, F32, SHR + 6656)
    cnb2 = sb("cnb2", 128, 1, F32, SHR + 7168)
    colsc2 = sb("colsc2", 128, 4, F32, SHR + 7232)
    wobf = sb("wobf", 128, 4096, BF16, SHR)
    lgT = sb("lgT", 8, T, F32, SHR)
    pa = [sb(f"pa{i}", 128, 512, BF16, SHR + i * 1024) for i in range(2)]
    pb = [sb(f"pb{i}", 128, 512, BF16, SHR + 2048 + i * 1024) for i in range(2)]
    linv = sb("linv", 1, 512, BF16, SHR + 4096)
    linvb = sb("linvb", 64, 512, F32, SHR + 5152)
    cwT = sb("cwT", 8, T, BF16, SHR)
    cwbT = sb("cwbT", 128, TS, BF16, SHR + 4096)
    gelt = [sb(f"gelt{i}", 128, 512, BF16, SHR + 6144 + i * 1024)
            for i in range(2)]
    ST = SHR + 8192
    w1f = [sb(f"w1f{i}", 128, 512, F32, ST + i * 2048) for i in range(2)]
    w2f = [sb(f"w2f{i}", 128, 512, F32, ST + 4096 + i * 2048) for i in range(2)]
    w1b = [sb(f"w1b{i}", 128, 1024, BF16, ST + 8192 + i * 2048) for i in range(2)]
    w2b = [sb(f"w2b{i}", 128, 1024, BF16, ST + 12288 + i * 2048) for i in range(2)] \
        + [sb(f"w2b{j+2}", 128, 1024, BF16, ST + 4096 + j * 2048) for j in range(2)]
    awf = [sb(f"awf{i}", 128, 1280, F32, ST + i * 5120) for i in range(2)]
    awb = [sb(f"awb{i}", 128, 1280, BF16, ST + 10240 + i * 2560) for i in range(2)]
    dnst = sb("dnst", 128, 2048, F32, ST)
    dnwb = sb("dnwb", 128, 4096, BF16, ST + 8192)
    b2f = sb("b2f", 8, 1024, F32, ST + 4096)
    xrb = [sb(f"xrb{i}", 128, 512, F32, ST + i * 2048) for i in range(2)]
    outb = [sb(f"outb{i}", 128, 512, F32, ST + 4096 + i * 2048) for i in range(2)]
    auto_ptr[0] = ST + 16384
    lnt = sb("lnt", 128, 512, F32)
    r_mean = sb("r_mean", 1, 512, F32, ST)
    r_aux = sb("r_aux", 1, 512, F32, ST + 2048)
    r_rstd = sb("r_rstd", 1, 512, BF16, ST + 4096)
    r_nmr = sb("r_nmr", 1, 512, BF16, ST + 5120)
    Xt = [sb(f"Xt{i}", 128, 128, F32) for i in range(2)]
    Yt = [sb(f"Yt{i}", 128, 128, F32) for i in range(2)]
    Rt = [sb(f"Rt{i}", 128, 256, F32) for i in range(2)]
    CBt = sb("CBt", 128, 128, F32)
    DIF = sb("DIF", 128, 128, F32)
    DPS = sb("DPS", 128, 128, F32)
    colsc = sb("colsc", 128, 4, F32)
    w1b.extend([sb(f"w1b{j+2}", 128, 1024, BF16, Xt[0].off + j * 2048)
                for j in range(2)])
    abt = sb("abt", 128, 2 * NT, F32)
    lat = sb("lat", 128, NT, F32)
    cumc = sb("cumc", 128, NT, F32)
    gnc = sb("gnc", 128, NT, F32)
    gmc = sb("gmc", 128, NT, F32)
    cnb = sb("cnb", 128, 1, F32)
    eye = sb("eye", 128, 128, F32)
    utones = sb("utones", 128, 128, F32)
    nbut = sb("nbut", 128, 128, F32)
    nblt = sb("nblt", 128, 128, F32)
    onesf = sb("onesf", 128, 128, F32)
    onesb = sb("onesb", 128, 512, BF16)
    causw = sb("causw", 128, 896, BF16)
    csel = sb("csel", 8, 1024, BF16)
    ccol = sb("ccol", 128, 8, F32)
    dnb = sb("dnb", 128, 4, F32)
    dnbo = sb("dnbo", 128, KC, F32)
    lnp = sb("lnp", 128, 4 * LAYERS * KC, F32)
    atbt = sb("atbt", 128, 3 * LAYERS * KC, F32)
    gatew = sb("gatew", 128, LAYERS * KC * E, BF16)
    gateb = sb("gateb", 1, LAYERS * E, BF16)
    b1b = [sb(f"b1b{i}", 128, NDFF, F32) for i in range(2)]
    b2t = sb("b2t", 8, 1024, BF16)
    lgtok = sb("lgtok", 128, NT * 8, F32)
    cwtok = sb("cwtok", 128, NT * 8, F32)
    tk = sb("tk", 128, 32, F32)
    cwcols = sb("cwcols", 128, 8, F32)
    cwsr = sb("cwsr", 1, 16, F32)
    cwst8 = sb("cwst8", 8, 2, F32)
    cwt_seq = sb("cwt_seq", 8, 512, BF16)
    hsum = sb("hsum", 128, KC, F32)
    hmean = sb("hmean", 128, KC, BF16)
    dnab = sb("dnab", 128, 1170, BF16)

    pp = [P.psum(f"pp{i}") for i in range(8)]

    def mm(psv, lhsT, rhs, start, stop, tp=None):
        return P.op("tensor",
                    (lambda e, o=psv.ap, a=lhsT.ap, b=rhs.ap, s1=start,
                     s2=stop, t=tp:
                     e.matmul(o, a, b, start=s1, stop=s2, tile_position=t)),
                    reads=[lhsT, rhs], writes=[psv])

    def tr(psv, inv, ident=None):
        idv = ident if ident is not None else eye.v
        return P.op("tensor",
                    (lambda e, o=psv.ap, a=inv.ap, i=idv.ap:
                     e.transpose(o, a, i)),
                    reads=[inv, idv], writes=[psv])

    def act(outv, inv, func, bias=0.0, scale=1.0):
        br = [bias] if isinstance(bias, View) else []
        bap = bias.ap if isinstance(bias, View) else bias
        return P.op("scalar",
                    (lambda e, o=outv.ap, i=inv.ap, f=func, b=bap, s=scale:
                     e.activation(o, i, f, bias=b, scale=s)),
                    reads=[inv] + br, writes=[outv])

    def vv(fname, outv, *ins, **kw):
        rd = [i for i in ins if isinstance(i, View)]
        aps = [i.ap if isinstance(i, View) else i for i in ins]
        return P.op("vector",
                    (lambda e, o=outv.ap, a=aps, f=fname, k=kw:
                     getattr(e, f)(o, *a, **k)),
                    reads=rd, writes=[outv])

    def ts2(outv, in0, s1, s2, op0, op1=None):
        rd = [in0] + [s for s in (s1, s2) if isinstance(s, View)]
        a1 = s1.ap if isinstance(s1, View) else s1
        a2 = s2.ap if isinstance(s2, View) else s2
        if op1 is None:
            return P.op("vector",
                        (lambda e, o=outv.ap, i=in0.ap, x=a1, y=a2, p0=op0:
                         e.tensor_scalar(o, i, x, y, p0)),
                        reads=rd, writes=[outv])
        return P.op("vector",
                    (lambda e, o=outv.ap, i=in0.ap, x=a1, y=a2, p0=op0, p1=op1:
                     e.tensor_scalar(o, i, x, y, p0, p1)),
                    reads=rd, writes=[outv])

    def stt(outv, in0, scal, in1, op0, op1):
        rd = [in0, in1] + ([scal] if isinstance(scal, View) else [])
        sap = scal.ap if isinstance(scal, View) else scal
        return P.op("vector",
                    (lambda e, o=outv.ap, i0=in0.ap, s=sap, i1=in1.ap, p0=op0,
                     p1=op1: e.scalar_tensor_tensor(o, i0, s, i1, p0, p1)),
                    reads=rd, writes=[outv])

    def conv(engname, outv, inv):
        if engname == "scalar":
            return act(outv, inv, AF.Identity)
        return P.op(engname,
                    (lambda e, o=outv.ap, i=inv.ap: e.tensor_copy(o, i)),
                    reads=[inv], writes=[outv])

    def memset(eng, outv, val):
        return P.op(eng, (lambda e, o=outv.ap, v=val: e.memset(o, v)),
                    reads=[], writes=[outv])

    for nm, tl in [("c_eye", eye), ("c_ut_ones", utones), ("c_nb_utinc", nbut),
                   ("c_nb_ltstrict", nblt), ("c_onesf", onesf),
                   ("c_onesb", onesb), ("c_causw", causw), ("c_csel", csel),
                   ("c_col", ccol), ("dn_bqkvg_r", dnb), ("dn_bo_fm", dnbo),
                   ("lnp_fm", lnp), ("atb_fm", atbt)]:
        P.dma("sync", tl.v, dram[nm], family="init")
    for kc in range(KC):
        P.dma("sync", xa[:, kc * T:(kc + 1) * T],
              dram["x_fm"][kc * 128:(kc + 1) * 128, :], family=f"xl{kc % 4}")
    P.dma("sync", dnst[:, 0:LAYERS * KC * E], dram["gate_w_fm"], family="init")
    P.dma("sync", dnst[0:1, 1024:1024 + LAYERS * E], dram["gate_b_row"],
          family="init")
    conv("vector", gatew.v, dnst[:, 0:LAYERS * KC * E])
    conv("vector", gateb.v, dnst[0:1, 1024:1024 + LAYERS * E])

    def _ckpt_init(ck):
        pass

    def lnw(which, l, kc):
        return lnp[:, (which * LAYERS + l) * KC + kc]

    def atb(which, l, kc):
        return atbt[:, (which * LAYERS + l) * KC + kc]

    def layer_norm(dst_tile, wb):
        for sl in range(NS):
            t0 = sl * 512
            pS = pp[6] if sl % 2 == 0 else pp[4]
            pQ = pp[7] if sl % 2 == 0 else pp[5]
            for kc in range(KC):
                i = (sl * KC + kc) % 4
                conv("vector", xbf[:, i * 512:(i + 1) * 512],
                     xa[:, kc * T + t0:kc * T + t0 + 512])
                act(xsq[:, i * 512:(i + 1) * 512],
                    xbf[:, i * 512:(i + 1) * 512], AF.Square)
                mm(pS[0:1, 0:512], onesb[:, 0:1],
                   xbf[:, i * 512:(i + 1) * 512], start=(kc == 0),
                   stop=(kc == KC - 1))
                mm(pQ[0:1, 0:512], onesb[:, 0:1],
                   xsq[:, i * 512:(i + 1) * 512], start=(kc == 0),
                   stop=(kc == KC - 1))
            act(r_mean.v, pS[0:1, 0:512], AF.Identity, scale=1.0 / D)
            act(r_aux.v, pQ[0:1, 0:512], AF.Identity, scale=1.0 / D)
            vv("tensor_mul", lnt[0:1, 0:512], r_mean.v, r_mean.v)
            vv("tensor_sub", r_aux.v, r_aux.v, lnt[0:1, 0:512])
            act(lnt[0:1, 0:512], r_aux.v, AF.Ln, bias=ccol[0:1, 0])
            act(r_rstd.v, lnt[0:1, 0:512], AF.Exp, scale=-0.5)
            act(lnt[0:1, 0:512], r_rstd.v, AF.Identity)
            vv("tensor_mul", lnt[0:1, 0:512], r_mean.v, lnt[0:1, 0:512])
            ts2(r_nmr.v, lnt[0:1, 0:512], -1.0, None, ALU.mult)
            mm(pS[0:128, 0:512], onesb[0:1, 0:128], r_rstd.v,
               start=True, stop=True)
            mm(pQ[0:128, 0:512], onesb[0:1, 0:128], r_nmr.v,
               start=True, stop=True)
            for kc in range(KC):
                vv("tensor_mul", lnt.v, xa[:, kc * T + t0:kc * T + t0 + 512],
                   pS[0:128, 0:512])
                dst = dst_tile[:, kc * T + t0:kc * T + t0 + 512]
                if wb is None:
                    vv("tensor_add", dst, lnt.v, pQ[0:128, 0:512])
                else:
                    l, which = wb
                    vv("tensor_add", lnt.v, lnt.v, pQ[0:128, 0:512])
                    ts2(dst, lnt.v, lnw(0 if which == 1 else 2, l, kc),
                        lnw(1 if which == 1 else 3, l, kc), ALU.mult, ALU.add)

    def ckpt(name):
        if STOP_AFTER == name:
            raise _StopBuild()

    def _build_body(ckpt):
        XtS, YtS, RtS = [Xt, Xt2], [Yt, Yt2], [Rt, Rt2]
        CBtS, DIFS, DPSS = [CBt, CBt2], [DIF, DIF2], [DPS, DPS2]
        cumrS, cnbS, colscS = [cumr, cumr2], [cnb, cnb2], [colsc, colsc2]
        ckpt("init0")
        layer_norm(bfA, None)
        ckpt("lnA")

        for half in range(2):
            P.dma("sync", dnst.v,
                  dram["dn_wqkvg_r"][:, half * 2048:half * 2048 + 2048],
                  family="dnw")
            conv("vector", dnwb[:, half * 2048:half * 2048 + 1024],
                 dnst[:, 0:1024])
            conv("scalar", dnwb[:, half * 2048 + 1024:half * 2048 + 2048],
                 dnst[:, 1024:2048])
        P.dma("sync", dnst[:, 0:1170], dram["dn_vab_r"], family="dnw")
        conv("vector", dnab.v, dnst[:, 0:1170])

        def dnw_chunk(w, kc):
            return dnwb[:, w * 1024 + kc * 128:w * 1024 + kc * 128 + 128]

        for sl in range(NS):
            t0 = sl * 512
            for kcs in range(KC):
                mm(pp[0][0:128, 0:512], dnw_chunk(1, kcs),
                   bfA[:, kcs * T + t0:kcs * T + t0 + 512],
                   start=(kcs == 0), stop=(kcs == KC - 1))
            act(khat[:, t0:t0 + 512], pp[0][0:128, 0:512], AF.Identity, bias=dnb[:, 1])
            for tt2 in range(4):
                ttg = sl * 4 + tt2
                tb = ttg * 128
                for kcs in range(KC):
                    mm(pp[1][0:128, 0:130], bfA[:, kcs * T + tb:kcs * T + tb + 128],
                       dnab[:, kcs * 130:kcs * 130 + 130],
                       start=(kcs == 0), stop=False)
                mm(pp[1][0:128, 0:130], onesb[:, 0:128],
                   dnab[:, 8 * 130:8 * 130 + 130], start=False, stop=True)
                act(vtok[:, tb:tb + 128], pp[1][0:128, 0:128], AF.Identity)
                act(abt[:, ttg * 2:ttg * 2 + 2], pp[1][0:128, 128:130],
                    AF.Sigmoid)
        for sl in range(NS):
            t0 = sl * 512
            for kcs in range(KC):
                mm(pp[sl][0:128, 0:512], dnw_chunk(0, kcs),
                   bfA[:, kcs * T + t0:kcs * T + t0 + 512],
                   start=(kcs == 0), stop=(kcs == KC - 1))
        for sl in range(NS):
            t0 = sl * 512
            for kcs in range(KC):
                mm(pp[4 + sl][0:128, 0:512], dnw_chunk(3, kcs),
                   bfA[:, kcs * T + t0:kcs * T + t0 + 512],
                   start=(kcs == 0), stop=(kcs == KC - 1))
        for sl in range(NS):
            t0 = sl * 512
            act(qT[:, t0:t0 + 512], pp[sl][0:128, 0:512], AF.Identity, bias=dnb[:, 0])
            act(gTt[:, t0:t0 + 512], pp[4 + sl][0:128, 0:512], AF.Sigmoid,
                bias=dnb[:, 3])

        for sl in range(NS):
            t0 = sl * 512
            act(lnt.v, khat[:, t0:t0 + 512], AF.Square)
            mm(pp[0][0:1, 0:512], onesf[:, 0:1], lnt.v, start=True, stop=True)
            act(r_aux.v, pp[0][0:1, 0:512], AF.Ln)
            act(r_mean.v, r_aux.v, AF.Exp, scale=-0.5)
            ts2(r_mean.v, r_mean.v, ccol[0:1, 1], None, ALU.min)
            conv("vector", r_rstd.v, r_mean.v)
            mm(pp[0][0:128, 0:512], onesb[0:1, 0:128], r_rstd.v, start=True,
               stop=True)
            vv("tensor_mul", khat[:, t0:t0 + 512], khat[:, t0:t0 + 512],
               pp[0][0:128, 0:512])

        for cc in range(NT):
            bb = 4 if cc % 2 == 0 else 0
            cumrL, cnbL = cumrS[cc % 2], cnbS[cc % 2]
            act(lat[:, cc], abt[:, 2 * cc], AF.Ln)
            mm(pp[bb + 1][0:128, 0:1], utones.v, lat[:, cc], start=True, stop=True)
            conv("vector", cumc[:, cc], pp[bb + 1][0:128, 0:1])
            act(gmc[:, cc], cumc[:, cc], AF.Exp)
            mm(pp[bb + 1][0:1, 0:128], lat[:, cc], utones.v, start=True, stop=True)
            conv("vector", cumrL.v, pp[bb + 1][0:1, 0:128])
            mm(pp[bb + 2][0:128, 0:1], onesf[0:1, 0:128], cumrL[0:1, 127],
               start=True, stop=True)
            conv("vector", cnbL.v, pp[bb + 2][0:128, 0:1])
            P.op("scalar",
                 (lambda e, o=gnc[:, cc].ap, i=cumc[:, cc].ap, b=cnbL.v.ap:
                  e.activation(o, i, AF.Exp, bias=b, scale=-1.0)),
                 reads=[cumc[:, cc], cnbL.v], writes=[gnc[:, cc]])

        ckpt("proj")
        for cc in range(NT):
            cb = cc * 128
            bb = 4 if cc % 2 == 0 else 0
            XtL, YtL, RtL = XtS[cc % 2], YtS[cc % 2], RtS[cc % 2]
            CBtL, DIFL, DPSL = CBtS[cc % 2], DIFS[cc % 2], DPSS[cc % 2]
            cumrL, colscL = cumrS[cc % 2], colscS[cc % 2]
            ksl = khat[:, cb:cb + 128]
            tr(pp[bb][0:128, 0:128], ksl)
            conv("vector", ktok[:, cb:cb + 128], pp[bb][0:128, 0:128])
            mm(pp[bb + 1][0:128, 0:128], ksl, ksl, start=True, stop=True)
            mm(pp[bb + 2][0:1, 0:128], lat[:, cc], utones.v, start=True, stop=True)
            conv("vector", cumrL.v, pp[bb + 2][0:1, 0:128])
            mm(pp[bb + 2][0:128, 0:128], onesf[0:1, 0:128], cumrL.v, start=True,
               stop=True)
            ts2(CBtL.v, onesf.v, cumc[:, cc], None, ALU.mult)
            vv("tensor_sub", DIFL.v, CBtL.v, pp[bb + 2][0:128, 0:128])
            vv("tensor_add", DIFL.v, DIFL.v, nbut.v)
            act(DPSL.v, DIFL.v, AF.Exp)
            ts2(DPSL.v, DPSL.v, abt[:, 2 * cc + 1], -1.0, ALU.mult, ALU.mult)
            vv("tensor_mul", XtL[0].v, pp[bb + 1][0:128, 0:128], DPSL.v)
            tr(pp[bb + 1][0:128, 0:128], XtL[0].v)
            conv("vector", YtL[0].v, pp[bb + 1][0:128, 0:128])
            vv("tensor_mul", colscL[:, 0], abt[:, 2 * cc + 1], gmc[:, cc])
            ts2(RtL[0][:, 0:128], vtok[:, cb:cb + 128], abt[:, 2 * cc + 1], None,
                ALU.mult)
            ts2(RtL[0][:, 128:256], ktok[:, cb:cb + 128], colscL[:, 0], None,
                ALU.mult)
            cur = 0
            for j in range(7):
                mm(pp[bb + 2][0:128, 0:256], YtL[cur].v, RtL[cur].v, start=True,
                   stop=True)
                vv("tensor_add", RtL[1 - cur].v, RtL[cur].v,
                   pp[bb + 2][0:128, 0:256])
                if j < 6:
                    mm(pp[bb + 3][0:128, 0:128], YtL[cur].v, XtL[cur].v,
                       start=True, stop=True)
                    mm(pp[bb + 1][0:128, 0:128], XtL[cur].v, YtL[cur].v,
                       start=True, stop=True)
                    conv("vector", XtL[1 - cur].v, pp[bb + 3][0:128, 0:128])
                    conv("vector", YtL[1 - cur].v, pp[bb + 1][0:128, 0:128])
                cur = 1 - cur
            conv("vector", uv[:, cb:cb + 128], RtL[cur][:, 0:128])
            tr(pp[bb][0:128, 0:128], RtL[cur][:, 128:256])
            conv("vector", wt[:, cb:cb + 128], pp[bb][0:128, 0:128])
            ts2(DPSL.v, RtL[cur][:, 128:256], gnc[:, cc], -1.0, ALU.mult,
                ALU.mult)
            mm(pp[bb + 2][0:128, 0:128], DPSL.v, ktok[:, cb:cb + 128], start=True,
               stop=True)
            vv("tensor_mul", colscL[:, 1], gnc[:, cc], gmc[:, cc])
            stt(et[:, cb:cb + 128], eye.v, colscL[:, 1], pp[bb + 2][0:128, 0:128],
                ALU.mult, ALU.add)

        ckpt("ph1")
        memset("vector", hS[:, 0:128], 0.0)
        memset("vector", htS[:, 0:128], 0.0)
        for cc in range(NT - 1):
            cb = cc * 128
            bb = 4 if cc % 2 == 0 else 0
            XtL = XtS[cc % 2]
            ts2(XtL[0].v, uv[:, cb:cb + 128], gnc[:, cc], None, ALU.mult)
            ts2(XtL[1].v, ktok[:, cb:cb + 128], gnc[:, cc], None, ALU.mult)
            mm(pp[bb + 3][0:128, 0:128], et[:, cb:cb + 128], hS[:, cb:cb + 128],
               start=True, stop=False)
            mm(pp[bb + 3][0:128, 0:128], XtL[1].v, XtL[0].v, start=False,
               stop=True)
            conv("vector", hS[:, cb + 128:cb + 256], pp[bb + 3][0:128, 0:128])
            tr(pp[bb + 2][0:128, 0:128], hS[:, cb + 128:cb + 256])
            conv("vector", htS[:, cb + 128:cb + 256], pp[bb + 2][0:128, 0:128])

        ckpt("ph2")
        for cc in range(NT):
            cb = cc * 128
            bb = 4 if cc % 2 == 0 else 0
            XtL, YtL = XtS[cc % 2], YtS[cc % 2]
            CBtL, DIFL, DPSL = CBtS[cc % 2], DIFS[cc % 2], DPSS[cc % 2]
            cumrL = cumrS[cc % 2]
            tr(pp[bb][0:128, 0:128], uv[:, cb:cb + 128])
            conv("vector", XtL[0].v, pp[bb][0:128, 0:128])
            mm(pp[bb + 1][0:128, 0:128], hS[:, cb:cb + 128], wt[:, cb:cb + 128],
               start=True, stop=True)
            vv("tensor_sub", XtL[0].v, XtL[0].v, pp[bb + 1][0:128, 0:128])
            mm(pp[bb + 2][0:1, 0:128], lat[:, cc], utones.v, start=True, stop=True)
            conv("vector", cumrL.v, pp[bb + 2][0:1, 0:128])
            mm(pp[bb + 2][0:128, 0:128], onesf[0:1, 0:128], cumrL.v, start=True,
               stop=True)
            act(YtL[1].v, pp[bb + 2][0:128, 0:128], AF.Exp)
            vv("tensor_mul", YtL[0].v, qT[:, cb:cb + 128], YtL[1].v)
            ts2(CBtL.v, onesf.v, cumc[:, cc], None, ALU.mult)
            vv("tensor_sub", DIFL.v, pp[bb + 2][0:128, 0:128], CBtL.v)
            vv("tensor_add", DIFL.v, DIFL.v, nblt.v)
            act(DPSL.v, DIFL.v, AF.Exp)
            mm(pp[bb + 3][0:128, 0:128], XtL[0].v, qT[:, cb:cb + 128], start=True,
               stop=True)
            vv("tensor_mul", XtL[1].v, pp[bb + 3][0:128, 0:128], DPSL.v)
            mm(pp[bb][0:128, 0:128], htS[:, cb:cb + 128], YtL[0].v, start=True,
               stop=False)
            mm(pp[bb][0:128, 0:128], ktok[:, cb:cb + 128], XtL[1].v, start=False,
               stop=True)
            vv("tensor_mul", og[:, cb:cb + 128], pp[bb][0:128, 0:128],
               gTt[:, cb:cb + 128])

        ckpt("ph3")
        P.dma("sync", dnst[:, 0:1024], dram["dn_wo_r"], family="dnw")
        conv("vector", dnwb[:, 0:1024], dnst[:, 0:1024])
        for dc in range(KC):
            for sl in range(NS):
                t0 = sl * 512
                psv = pp[(dc * NS + sl) % 4]
                mm(psv[0:128, 0:512], dnwb[:, dc * 128:dc * 128 + 128],
                   og[:, t0:t0 + 512], start=True, stop=True)
                stt(xa[:, dc * T + t0:dc * T + t0 + 512], psv[0:128, 0:512],
                    dnbo[:, dc], xa[:, dc * T + t0:dc * T + t0 + 512],
                    ALU.add, ALU.add)

        ckpt("stageA")
        for kc in range(KC):
            d = P.dma("gpsimd", xres_d[kc * 128:(kc + 1) * 128, :],
                      xa[:, kc * T:(kc + 1) * T], family="spl")
            P.dram_touch(d, "xres", True)
        layer_norm(xa, None)
        ckpt("ln0")

        for l in range(LAYERS):
            layer_norm(bfA, (l, 1))
            for g in range(NG):
                for w_i, (wname, dst_t, bwhich) in enumerate(
                        [("at_wq_r", qt_g, 0), ("at_wk_r", kt_g, 1)]):
                    for half in range(2):
                        bi = (w_i * 2 + half) % 2
                        P.dma("sync", awf[bi][:, 0:1024],
                              dram[wname][l, g * 2 + half], family=f"aw{bi}")
                        conv("vector" if bi == 0 else "scalar",
                             awb[bi][:, 0:1024], awf[bi][:, 0:1024])
                        for sl in range(NS):
                            t0 = sl * 512
                            psv = pp[w_i * 2 + (sl % 2)]
                            for kcs in range(KC):
                                mm(psv[0:128, 0:512],
                                   awb[bi][:, kcs * 128:kcs * 128 + 128],
                                   bfA[:, kcs * T + t0:kcs * T + t0 + 512],
                                   start=(kcs == 0), stop=(kcs == KC - 1))
                            act(dst_t[:, half * T + t0:half * T + t0 + 512],
                                psv[0:128, 0:512], AF.Identity,
                                bias=atb(bwhich, l, g * 2 + half))
                for half in range(2):
                    P.dma("sync", awf[half][:, 0:1170],
                          dram["at_wv9_r"][l, g, half], family=f"aw{half}")
                    conv("vector" if half == 0 else "scalar",
                         awb[half][:, 0:1170], awf[half][:, 0:1170])
                for tt2 in range(NT):
                    tb = tt2 * 128
                    for half in range(2):
                        psv = pp[4 + half]
                        for kcs in range(KC):
                            mm(psv[0:128, 0:130],
                               bfA[:, kcs * T + tb:kcs * T + tb + 128],
                               awb[half][:, kcs * 130:kcs * 130 + 130],
                               start=(kcs == 0), stop=False)
                        mm(psv[0:128, 0:130], onesb[:, 0:128],
                           awb[half][:, 8 * 130:8 * 130 + 130],
                           start=False, stop=True)
                        act(vt_g[:, tt2 * 260 + half * 130:
                                 tt2 * 260 + half * 130 + 130],
                            psv[0:128, 0:130], AF.Identity)
                ckpt(f"qkv{l}_{g}")
                for cpair in range(2):
                    cT = cpair * T
                    for qs in range(NS):
                        q0 = qs * 512
                        kmax = 4 * qs + 3
                        hA, hB = 2 * cpair, 2 * cpair + 1

                        def emit_scores(kt):
                            kb = kt * 128
                            sA = pp[0] if kt % 2 == 0 else pp[5]
                            sB = pp[1] if kt % 2 == 0 else pp[6]
                            mm(sA[0:128, 0:512],
                               kt_g[0:64, cT + kb:cT + kb + 128],
                               qt_g[0:64, cT + q0:cT + q0 + 512],
                               start=True, stop=True, tp=(0, 0))
                            mm(sB[0:128, 0:512],
                               kt_g[64:128, cT + kb:cT + kb + 128],
                               qt_g[64:128, cT + q0:cT + q0 + 512],
                               start=True, stop=True, tp=(64, 0))

                        def emit_exp(kt):
                            sA = pp[0] if kt % 2 == 0 else pp[5]
                            sB = pp[1] if kt % 2 == 0 else pp[6]
                            act(pa[kt % 2].v, sA[0:128, 0:512], AF.Exp,
                                scale=0.125)
                            act(pb[kt % 2].v, sB[0:128, 0:512], AF.Exp,
                                scale=0.125)
                            m = kt - 4 * qs
                            if m >= 0:
                                s0 = 384 - 128 * m
                                vv("tensor_mul", pa[kt % 2].v, pa[kt % 2].v,
                                   causw[:, s0:s0 + 512])
                                vv("tensor_mul", pb[kt % 2].v, pb[kt % 2].v,
                                   causw[:, s0:s0 + 512])

                        def emit_o(kt):
                            mm(pp[2][0:65, 0:512],
                               vt_g[:, kt * 260 + 65 * hA:kt * 260 + 65 * hA + 65],
                               pa[kt % 2].v, start=(kt == 0), stop=(kt == kmax))
                            mm(pp[3][0:65, 0:512],
                               vt_g[:, kt * 260 + 65 * hB:kt * 260 + 65 * hB + 65],
                               pb[kt % 2].v, start=(kt == 0), stop=(kt == kmax))

                        emit_scores(0)
                        emit_exp(0)
                        for kt in range(kmax + 1):
                            if kt + 1 <= kmax:
                                emit_scores(kt + 1)
                                emit_exp(kt + 1)
                            emit_o(kt)
                        gchunk = 2 * g + cpair
                        for hh, ppo in ((0, pp[2]), (1, pp[3])):
                            act(r_aux[0:1, 0:512], ppo[64:65, 0:512], AF.Ln)
                            act(linv.v, r_aux[0:1, 0:512], AF.Exp, scale=-1.0)
                            mm(pp[4][0:64, 0:512], onesb[0:1, 0:64], linv.v,
                               start=True, stop=True)
                            conv("vector", linvb.v, pp[4][0:64, 0:512])
                            vv("tensor_mul",
                               og[hh * 64:hh * 64 + 64,
                                  gchunk * T + q0:gchunk * T + q0 + 512],
                               ppo[0:64, 0:512], linvb.v)
            ckpt(f"scores{l}")
            for rnd in range(2):
                for kk in range(4):
                    kcs = rnd * 4 + kk
                    bi = kk % 2
                    P.dma("sync", awf[bi][:, 0:1024],
                          dram["at_wo_r"][l][:, kcs, :], family=f"aw{bi}")
                    conv("vector" if bi == 0 else "scalar",
                         wobf[:, kk * 1024:kk * 1024 + 1024], awf[bi][:, 0:1024])
                for dc in range(KC):
                    for sl in range(NS):
                        t0 = sl * 512
                        psv = pp[5 + ((dc * NS + sl) % 2)]
                        for kk in range(4):
                            kcs = rnd * 4 + kk
                            mm(psv[0:128, 0:512],
                               wobf[:, kk * 1024 + dc * 128:
                                    kk * 1024 + dc * 128 + 128],
                               og[:, kcs * T + t0:kcs * T + t0 + 512],
                               start=(kk == 0), stop=(kk == 3))
                        if rnd == 0:
                            stt(xa[:, dc * T + t0:dc * T + t0 + 512],
                                psv[0:128, 0:512], atb(2, l, dc),
                                xa[:, dc * T + t0:dc * T + t0 + 512],
                                ALU.add, ALU.add)
                        else:
                            stt(xa[:, dc * T + t0:dc * T + t0 + 512],
                                psv[0:128, 0:512], ccol[:, 3],
                                xa[:, dc * T + t0:dc * T + t0 + 512],
                                ALU.mult, ALU.add)

            ckpt(f"attn{l}")
            layer_norm(bfA, (l, 2))
            P.dma("sync", b2f.v, dram["e_b2"][l], family="b2l")
            conv("vector", b2t.v, b2f.v)
            if l % 2 == 0:
                for sl in range(NS):
                    t0 = sl * 512
                    for kcs in range(KC):
                        mm(pp[5][0:8, 0:512],
                           gatew[:, (l * KC + kcs) * E:(l * KC + kcs) * E + 8],
                           bfA[:, kcs * T + t0:kcs * T + t0 + 512],
                           start=(kcs == 0), stop=False)
                    mm(pp[5][0:8, 0:512], gateb[0:1, l * E:l * E + 8],
                       onesb[0:1, 0:512], start=False, stop=True)
                    act(lgT[0:8, t0:t0 + 512], pp[5][0:8, 0:512], AF.Identity)
                ckpt(f"lgT{l}")
                for tt2 in range(NT):
                    tb = tt2 * 128
                    tr(pp[6][0:128, 0:8], lgT[0:8, tb:tb + 128],
                       ident=eye[0:8, 0:8])
                    conv("vector", lgtok[:, tt2 * 8:tt2 * 8 + 8],
                         pp[6][0:128, 0:8])
                    lg8 = lgtok[:, tt2 * 8:tt2 * 8 + 8]
                    vv("reduce_max", tk[:, 0], lg8, axis=AX.X)
                    ts2(tk[:, 8:16], lg8, tk[:, 0], None, ALU.is_equal)
                    stt(tk[:, 16:24], tk[:, 8:16], ccol[:, 2], lg8, ALU.mult,
                        ALU.add)
                    vv("reduce_max", tk[:, 1], tk[:, 16:24], axis=AX.X)
                    ts2(tk[:, 24:32], tk[:, 16:24], tk[:, 1], None, ALU.is_equal)
                    vv("tensor_sub", tk[:, 2], tk[:, 1], tk[:, 0])
                    act(tk[:, 3], tk[:, 2], AF.Exp)
                    ts2(tk[:, 4], tk[:, 3], 1.0, None, ALU.add)
                    vv("reciprocal", tk[:, 4], tk[:, 4])
                    vv("tensor_mul", tk[:, 5], tk[:, 3], tk[:, 4])
                    ts2(cwtok[:, tt2 * 8:tt2 * 8 + 8], tk[:, 8:16], tk[:, 4],
                        None, ALU.mult)
                    stt(cwtok[:, tt2 * 8:tt2 * 8 + 8], tk[:, 24:32], tk[:, 5],
                        cwtok[:, tt2 * 8:tt2 * 8 + 8], ALU.mult, ALU.add)
                    tr(pp[6][0:8, 0:128], cwtok[:, tt2 * 8:tt2 * 8 + 8])
                    conv("vector", cwT[0:8, tb:tb + 128], pp[6][0:8, 0:128])
            else:
                for kcs in range(KC):
                    vv("reduce_sum", hsum[:, kcs], bfA[:, kcs * T:(kcs + 1) * T],
                       axis=AX.X)
                ts2(hmean.v, hsum.v, 1.0 / T, None, ALU.mult)
                for kcs in range(KC):
                    mm(pp[5][0:8, 0:1],
                       gatew[:, (l * KC + kcs) * E:(l * KC + kcs) * E + 8],
                       hmean[:, kcs], start=(kcs == 0), stop=False)
                mm(pp[5][0:8, 0:1], gateb[0:1, l * E:l * E + 8], onesb[0:1, 0:1],
                   start=False, stop=True)
                conv("vector", cwst8[0:8, 0], pp[5][0:8, 0:1])
                tr(pp[6][0:1, 0:8], cwst8[0:8, 0], ident=eye[0:8, 0:8])
                conv("vector", cwsr[0:1, 0:8], pp[6][0:1, 0:8])
                lg8 = cwsr[0:1, 0:8]
                vv("reduce_max", cwsr[0:1, 8], lg8, axis=AX.X)
                ts2(tk[0:1, 8:16], lg8, cwsr[0:1, 8], None, ALU.is_equal)
                stt(tk[0:1, 16:24], tk[0:1, 8:16], ccol[0:1, 2], lg8, ALU.mult,
                    ALU.add)
                vv("reduce_max", cwsr[0:1, 9], tk[0:1, 16:24], axis=AX.X)
                ts2(tk[0:1, 24:32], tk[0:1, 16:24], cwsr[0:1, 9], None,
                    ALU.is_equal)
                vv("tensor_sub", cwsr[0:1, 10], cwsr[0:1, 9], cwsr[0:1, 8])
                act(cwsr[0:1, 11], cwsr[0:1, 10], AF.Exp)
                ts2(cwsr[0:1, 12], cwsr[0:1, 11], 1.0, None, ALU.add)
                vv("reciprocal", cwsr[0:1, 12], cwsr[0:1, 12])
                vv("tensor_mul", cwsr[0:1, 13], cwsr[0:1, 11], cwsr[0:1, 12])
                ts2(tk[0:1, 0:8], tk[0:1, 8:16], cwsr[0:1, 12], None, ALU.mult)
                stt(tk[0:1, 0:8], tk[0:1, 24:32], cwsr[0:1, 13], tk[0:1, 0:8],
                    ALU.mult, ALU.add)
                for e in range(E):
                    mm(pp[6][0:128, 0:1], onesf[0:1, 0:128], tk[0:1, e],
                       start=True, stop=True)
                    conv("vector", cwcols[:, e], pp[6][0:128, 0:1])
                tr(pp[6][0:8, 0:1], tk[0:1, 0:8], ident=eye[0:1, 0:1])
                conv("vector", cwst8[0:8, 1], pp[6][0:8, 0:1])
                ts2(cwt_seq.v, onesb[0:8, 0:512], cwst8[0:8, 1], None, ALU.mult)

            ckpt(f"gate{l}")
            for p_i in range(NP):
                pt0 = p_i * TS
                for e in range(E):
                    if l % 2 == 0:
                        for s2 in range(NS2):
                            mm(pp[5][0:128, 0:512],
                               csel[0:8, e * 128:e * 128 + 128],
                               cwT[0:8, pt0 + s2 * 512:pt0 + s2 * 512 + 512],
                               start=True, stop=True)
                            act(cwbT[:, s2 * 512:s2 * 512 + 512],
                                pp[5][0:128, 0:512], AF.Identity)
                    P.dma("sync", b1b[e % 2].v, dram["e_b1_fm"][l, e],
                          family=f"b1{e % 2}")
                    for df in range(NDFF):
                        buf = df % 2
                        if p_i == 0:
                            for hf in range(2):
                                P.dma("sync", w1f[hf].v,
                                      dram["e_w1_r"][l, e, df]
                                      [:, hf * 512:hf * 512 + 512],
                                      family=f"w1f{hf}")
                                conv("vector" if hf == 0 else "scalar",
                                     w1b[buf][:, hf * 512:hf * 512 + 512],
                                     w1f[hf].v)
                            d = P.dma("gpsimd", w1bf_d[l, e, df], w1b[buf].v,
                                      family=f"w1s{df % 2}")
                            P.dram_touch(d, f"w1d{l}_{e}_{df}", True)
                        else:
                            d = P.dma("sync", w1b[buf].v, w1bf_d[l, e, df],
                                      family=f"w1r{buf}")
                            P.dram_touch(d, f"w1d{l}_{e}_{df}", False)
                        for s2 in range(NS2):
                            t0 = pt0 + s2 * 512
                            psv = pp[(df * NS2 + s2) % 4]
                            for kcs in range(KC):
                                mm(psv[0:128, 0:512],
                                   w1b[buf][:, kcs * 128:kcs * 128 + 128],
                                   bfA[:, kcs * T + t0:kcs * T + t0 + 512],
                                   start=(kcs == 0), stop=(kcs == KC - 1))
                            act(gelt[s2 % 2].v, psv[0:128, 0:512], AF.Gelu,
                                bias=b1b[e % 2][:, df])
                            h1d = h1p[:, df * TS + s2 * 512:df * TS + s2 * 512 + 512]
                            if l % 2 == 0:
                                vv("tensor_mul", h1d, gelt[s2 % 2].v,
                                   cwbT[:, s2 * 512:s2 * 512 + 512])
                            else:
                                ts2(h1d, gelt[s2 % 2].v, cwcols[:, e], None,
                                    ALU.mult)
                    wl2 = [w2b[0], w2b[1], w1b[2], w1b[3]]
                    for s2 in range(NS2):
                        t0 = pt0 + s2 * 512
                        cwsrc = (cwT[0:8, t0:t0 + 512] if l % 2 == 0
                                 else cwt_seq[0:8, 0:512])
                        if e == 0:
                            for dc in range(KC):
                                mm(pp[dc][0:128, 0:512],
                                   b2t[0:8, dc * 128:dc * 128 + 128], cwsrc,
                                   start=True, stop=False)
                        for k in range(NDFF):
                            wt2 = wl2[k % 4]
                            if p_i == 0 and s2 == 0:
                                for hf in range(2):
                                    P.dma("sync", w2f[hf].v,
                                          dram["e_w2"][l, e]
                                          [k * 128:(k + 1) * 128,
                                           hf * 512:hf * 512 + 512],
                                          family=f"w2f{hf}")
                                    conv("vector" if (2 * k + hf) % 2 == 0
                                         else "scalar",
                                         wt2[:, hf * 512:hf * 512 + 512],
                                         w2f[hf].v)
                                d = P.dma("gpsimd", w2bf_d[l, e, k],
                                          wt2.v, family=f"w2s{k % 2}")
                                P.dram_touch(d, f"w2d{l}_{e}_{k}", True)
                            else:
                                d = P.dma("sync", wt2.v, w2bf_d[l, e, k],
                                          family=f"w2r{k % 4}")
                                P.dram_touch(d, f"w2d{l}_{e}_{k}", False)
                            for dc in range(KC):
                                mm(pp[dc][0:128, 0:512],
                                   wt2[:, dc * 128:dc * 128 + 128],
                                   h1p[:, k * TS + s2 * 512:k * TS + s2 * 512 + 512],
                                   start=(k == 0 and e != 0), stop=(k == NDFF - 1))
                        for dc in range(KC):
                            stt(xa[:, dc * T + t0:dc * T + t0 + 512],
                                pp[dc][0:128, 0:512], ccol[:, 3],
                                xa[:, dc * T + t0:dc * T + t0 + 512],
                                ALU.mult, ALU.add)

        ckpt("moe")
        for kc in range(KC):
            for sl in range(NS):
                t0 = sl * 512
                i = (kc * NS + sl) % 2
                d = P.dma("sync", xrb[i].v,
                          xres_d[kc * 128:(kc + 1) * 128, t0:t0 + 512],
                          family=f"rld{i}")
                P.dram_touch(d, "xres", False)
                vv("tensor_add", outb[i].v, xrb[i].v,
                   xa[:, kc * T + t0:kc * T + t0 + 512])
                P.dma("gpsimd", out_fm[kc * 128:(kc + 1) * 128, t0:t0 + 512],
                      outb[i].v, family="out")

    try:
        _build_body(ckpt)
    except _StopBuild:
        pass

    for tapname, tname, p0, p1, f0, f1 in TAP_SPEC:
        tl = tiles[tname]
        tout = nc.dram_tensor(tapname, [p1 - p0, f1 - f0],
                              tl.dtype, kind="ExternalOutput").ap()
        P.dma("gpsimd", tout, tl[p0:p1, f0:f1], family="out")

    n_ops = P.build(final_wait_families=["out0", "out1", "out2", "out3", "out"])
    return nc, n_ops


def _prep_shared(i):
    f32 = np.float32
    bf = np.dtype("bfloat16") if False else None
    d = {}
    Wq, Wk, Wv, Wg = (np.asarray(i["L0_attn_Wq"]), np.asarray(i["L0_attn_Wk"]),
                      np.asarray(i["L0_attn_Wv"]), np.asarray(i["L0_attn_Wg"]))
    wqkvg = np.stack([Wq, Wk, Wv, Wg])
    d["dn_wqkvg_r"] = np.ascontiguousarray(
        wqkvg.reshape(4, KC, 128, 128).transpose(2, 0, 1, 3)
        .reshape(128, 4 * KC * 128)).astype(f32)
    d["dn_bqkvg_r"] = np.ascontiguousarray(np.stack(
        [np.asarray(i["L0_attn_bq"]), np.asarray(i["L0_attn_bk"]),
         np.asarray(i["L0_attn_bv"]), np.asarray(i["L0_attn_bg"])], axis=1)
    ).astype(f32)
    vab9 = np.zeros((9, 128, 130), f32)
    vab9[0:8, :, 0:128] = np.asarray(i["L0_attn_Wv"]).reshape(8, 128, 128)
    vab9[8, 0, 0:128] = np.asarray(i["L0_attn_bv"])
    vab9[0:8, :, 128:130] = np.concatenate(
        [np.asarray(i["L0_attn_Wa"]), np.asarray(i["L0_attn_Wb"])],
        axis=1).reshape(8, 128, 2)
    vab9[8, 0, 128] = np.asarray(i["L0_attn_ba"])[0]
    vab9[8, 0, 129] = np.asarray(i["L0_attn_bb"])[0]
    d["dn_vab_r"] = np.ascontiguousarray(
        vab9.transpose(1, 0, 2).reshape(128, 9 * 130)).astype(f32)
    d["dn_wo_r"] = np.ascontiguousarray(np.asarray(i["L0_attn_Wo"])).astype(f32)
    d["dn_bo_fm"] = np.ascontiguousarray(
        np.asarray(i["L0_attn_bo"]).reshape(KC, 128).T).astype(f32)
    wq = np.asarray(i["L0_ffn_wq"])
    wk = np.asarray(i["L0_ffn_wk"])
    wv = np.asarray(i["L0_ffn_wv"])
    wo = np.asarray(i["L0_ffn_wo"])
    d["at_wq_r"] = np.ascontiguousarray(
        wq.reshape(LAYERS, KC, 128, KC, 128).transpose(0, 3, 2, 1, 4)
        .reshape(LAYERS, KC, 128, KC * 128)).astype(f32)
    d["at_wk_r"] = np.ascontiguousarray(
        wk.reshape(LAYERS, KC, 128, KC, 128).transpose(0, 3, 2, 1, 4)
        .reshape(LAYERS, KC, 128, KC * 128)).astype(f32)
    bv = np.asarray(i["L0_ffn_bv"])
    wv9 = np.zeros((LAYERS, NG, 2, 9, 128, 130), f32)
    for l in range(LAYERS):
        for g in range(NG):
            for half in range(2):
                for hloc in range(2):
                    h = 4 * g + half * 2 + hloc
                    cols = slice(hloc * 65, hloc * 65 + 64)
                    wcol = wv[l][:, h * 64:(h + 1) * 64]
                    wv9[l, g, half, 0:8, :, cols] = wcol.reshape(KC, 128, 64)
                    wv9[l, g, half, 8, 0, cols] = bv[l][h * 64:(h + 1) * 64]
                    wv9[l, g, half, 8, 0, hloc * 65 + 64] = 1.0
    d["at_wv9_r"] = np.ascontiguousarray(
        wv9.transpose(0, 1, 2, 4, 3, 5).reshape(LAYERS, NG, 2, 128, 9 * 130)
    ).astype(f32)
    d["at_wo_r"] = np.ascontiguousarray(
        wo.reshape(LAYERS, KC, 128, D).transpose(0, 2, 1, 3)).astype(f32)
    fm = lambda a: np.ascontiguousarray(
        np.asarray(a).reshape(LAYERS, KC, 128).transpose(2, 0, 1)
        .reshape(128, LAYERS * KC))
    d["atb_fm"] = np.ascontiguousarray(np.concatenate(
        [fm(i["L0_ffn_bq"]), fm(i["L0_ffn_bk"]), fm(i["L0_ffn_bo"])],
        axis=1)).astype(f32)
    d["lnp_fm"] = np.ascontiguousarray(np.concatenate(
        [fm(i["L0_ffn_ln1_w"]), fm(i["L0_ffn_ln1_b"]),
         fm(i["L0_ffn_ln2_w"]), fm(i["L0_ffn_ln2_b"])], axis=1)).astype(f32)
    gw = np.asarray(i["L0_ffn_gate_w"])
    d["gate_w_fm"] = np.ascontiguousarray(
        gw.reshape(LAYERS, KC, 128, E).transpose(2, 0, 1, 3)
        .reshape(128, LAYERS * KC * E)).astype(f32)
    d["gate_b_row"] = np.ascontiguousarray(
        np.asarray(i["L0_ffn_gate_b"]).reshape(1, LAYERS * E)).astype(f32)
    ew1 = np.asarray(i["L0_ffn_e_w1"])
    d["e_w1_r"] = np.ascontiguousarray(
        ew1.reshape(LAYERS, E, KC, 128, NDFF, 128).transpose(0, 1, 4, 3, 2, 5)
        .reshape(LAYERS, E, NDFF, 128, KC * 128)).astype(f32)
    eb1 = np.asarray(i["L0_ffn_e_b1"])
    d["e_b1_fm"] = np.ascontiguousarray(
        eb1.reshape(LAYERS, E, NDFF, 128).transpose(0, 1, 3, 2)).astype(f32)
    d["e_w2"] = np.ascontiguousarray(np.asarray(i["L0_ffn_e_w2"])).astype(f32)
    d["e_b2"] = np.ascontiguousarray(np.asarray(i["L0_ffn_e_b2"])).astype(f32)
    d["c_eye"] = np.eye(128, dtype=f32)
    d["c_ut_ones"] = np.triu(np.ones((128, 128), f32))
    d["c_nb_utinc"] = np.where(np.triu(np.ones((128, 128), bool)),
                               f32(NEGBIG), f32(0.0)).astype(f32)
    d["c_nb_ltstrict"] = np.where(np.tril(np.ones((128, 128), bool), -1),
                                  f32(NEGBIG), f32(0.0)).astype(f32)
    d["c_onesf"] = np.ones((128, 128), f32)
    import ml_dtypes
    bf16 = ml_dtypes.bfloat16
    d["c_onesb"] = np.ones((128, 512), bf16)
    rr = np.arange(128)[:, None]
    cc = np.arange(896)[None, :]
    d["c_causw"] = (rr <= (cc - 384)).astype(bf16)
    cs = np.zeros((8, 1024), np.float32)
    for e in range(E):
        cs[e, e * 128:(e + 1) * 128] = 1.0
    d["c_csel"] = cs.astype(bf16)
    col = np.zeros((128, 8), f32)
    col[:, 0] = 1e-5
    col[:, 1] = 1e12
    col[:, 2] = NEGBIG
    col[:, 3] = 1.0
    d["c_col"] = col
    return d


_CACHED = {}


def kernel(**inputs):
    T = T_FULL
    x = np.asarray(inputs["L0_attn_x"])
    B = x.shape[0]
    shared = _prep_shared(inputs)
    nc, n_ops = build_program(T)
    in_maps = []
    for b in range(B):
        m = dict(shared)
        m["x_fm"] = np.ascontiguousarray(x[b].T.astype(np.float32))
        in_maps.append(m)
    res = run_bass_kernel_spmd(nc, in_maps, core_ids=list(range(B)))
    out = np.stack([np.ascontiguousarray(r["out_fm"].T)
                    for r in res.results], axis=0)
    return out.astype(np.float32)
